# version 01
# baseline (speedup 1.0000x reference)
import contextlib
import numpy as np
import ml_dtypes
import concourse.bass as bass
import concourse.mybir as mybir
from concourse.bass_utils import run_bass_kernel_spmd

F32 = mybir.dt.float32
BF16 = mybir.dt.bfloat16
AF = mybir.ActivationFunctionType
ALU = mybir.AluOpType
AXX = mybir.AxisListType.X

NCORES = 8
D = 1024
T = 2048
TH = T // 2
H = 4096
NF = D // 128
NKH = H // 128
NE = 8
PI = float(np.pi)
MAGIC = 12582912.0

BUILD_PHASES = 99


def _stage_vectors(weights, dim, transpose=False):
    out = []
    f = np.arange(dim)
    for stage in range(weights.shape[0]):
        s = 2 ** stage
        bit = (f // s) % 2
        k = (f // (2 * s)) * s + (f % s)
        w = weights[stage]
        if not transpose:
            W0 = np.where(bit == 0, w[k, 0, 0], w[k, 1, 1])
            W1 = np.where(bit == 0, w[k, 0, 1], w[k, 1, 0])
        else:
            W0 = np.where(bit == 0, w[k, 0, 0], w[k, 1, 1])
            W1 = np.where(bit == 0, w[k, 1, 0], w[k, 0, 1])
        out.append((W0.astype(np.float32), W1.astype(np.float32)))
    return out


def prep_inputs(inp):
    c32 = np.ascontiguousarray
    sh = {}
    svu = _stage_vectors(np.asarray(inp["L0_ffn_up_weights"], np.float32), H, False)
    svd = _stage_vectors(np.asarray(inp["L0_ffn_down_weights"], np.float32), H, True)
    sh["bfu"] = c32(np.stack([np.stack(p) for p in svu]))
    sh["bfd"] = c32(np.stack([np.stack(svd[s]) for s in reversed(range(12))]))
    sh["ident"] = np.eye(128, dtype=np.float32)
    sh["b_exp"] = c32(np.asarray(inp["L0_ffn_b_exp"], np.float32).reshape(1, H))
    sh["up_bias_r"] = c32(np.asarray(inp["L0_ffn_up_bias"], np.float32).reshape(NKH, 128).T)
    sh["b_con_r"] = c32(np.asarray(inp["L0_ffn_b_con"], np.float32).reshape(NF, 128).T)
    sh["down_bias_r"] = c32(np.asarray(inp["L0_ffn_down_bias"], np.float32).reshape(NF, 128).T)
    sh["l0_gamma_r"] = c32(np.asarray(inp["L0_attn_gamma"], np.float32).reshape(NF, 128).T)
    sh["wqkvaT"] = c32(np.asarray(inp["L0_attn_w_qkva"], np.float32).T)
    sh["w1T_l1a"] = c32(np.asarray(inp["L1_attn_w1"], np.float32).T)
    sh["w2T_l1a"] = c32(np.asarray(inp["L1_attn_w2"], np.float32).T)
    sh["b1r_l1a"] = c32(np.asarray(inp["L1_attn_b1"], np.float32).reshape(NKH, 128).T)
    sh["b2_l1a"] = c32(np.asarray(inp["L1_attn_b2"], np.float32).reshape(1, D))
    sh["gamma1"] = c32(np.asarray(inp["L1_attn_gamma1"], np.float32).reshape(1, D))
    sh["beta1"] = c32(np.asarray(inp["L1_attn_beta1"], np.float32).reshape(1, D))
    sh["gamma2_r"] = c32(np.asarray(inp["L1_attn_gamma2"], np.float32).reshape(NF, 128).T)
    sh["beta2_r"] = c32(np.asarray(inp["L1_attn_beta2"], np.float32).reshape(NF, 128).T)
    sh["routerT"] = c32(np.asarray(inp["L1_ffn_router_w"], np.float32).T)
    sh["langT"] = c32(np.asarray(inp["L1_ffn_lang_w"], np.float32).T)
    sh["router_b"] = c32(np.asarray(inp["L1_ffn_router_b"], np.float32).reshape(1, NE))
    sh["lang_b"] = c32(np.asarray(inp["L1_ffn_lang_b"], np.float32).reshape(1, 4))
    w1 = np.asarray(inp["L1_ffn_w1"], np.float32)
    sh["moe_w1_r"] = c32(w1.reshape(NE, D, NKH, 128).transpose(0, 2, 1, 3))
    sh["moe_w2"] = c32(np.asarray(inp["L1_ffn_w2"], np.float32))
    sh["moe_b1_r"] = c32(np.asarray(inp["L1_ffn_b1"], np.float32).reshape(NE, NKH, 128)
                         .transpose(2, 0, 1).reshape(128, NE * NKH))
    sh["moe_b2_r"] = c32(np.asarray(inp["L1_ffn_b2"], np.float32).reshape(NE, NF, 128)
                         .transpose(2, 0, 1).reshape(128, NE * NF))
    wexp = np.asarray(inp["L0_ffn_W_exp"], np.float32)
    wconT = c32(np.asarray(inp["L0_ffn_W_con"], np.float32).T)
    x = np.asarray(inp["L0_attn_x"], np.float32)
    per_core = []
    for c in range(NCORES):
        b, q = c // 4, c % 4
        m = dict(sh)
        m["xT"] = c32(x[b, q * T:(q + 1) * T, :].T)
        m["wexp_sl"] = c32(wexp[128 * c:128 * (c + 1), :])
        m["wconT_sl"] = c32(wconT[128 * c:128 * (c + 1), :])
        oh = np.zeros((1, 8), np.float32)
        oh[0, c] = 1.0
        m["onehot"] = oh
        per_core.append(m)
    return per_core


class Ctx:
    def __init__(self, nc, stack):
        self.nc = nc
        self.stack = stack
        self.sem = {}
        self.cnt = {}

    def new(self, name):
        self.sem[name] = self.stack.enter_context(self.nc.semaphore(name))
        self.cnt[name] = 0

    def inc(self, instr, name, n=1):
        instr.then_inc(self.sem[name], n)
        self.cnt[name] += n
        return self.cnt[name]

    def wait(self, eng, name, val=None):
        eng.wait_ge(self.sem[name], self.cnt[name] if val is None else val)


def build_program(build_phases=None):
    bp = BUILD_PHASES if build_phases is None else build_phases
    nc = bass.Bass("TRN2", target_bir_lowering=False)
    stack = contextlib.ExitStack()
    cx = Ctx(nc, stack)
    dr = {}

    def din(name, shape):
        dr[name] = nc.dram_tensor(name, list(shape), F32, kind="ExternalInput")

    din("xT", [D, T]); din("wexp_sl", [128, H]); din("wconT_sl", [128, H])
    din("bfu", [12, 2, H]); din("bfd", [12, 2, H]); din("ident", [128, 128])
    din("b_exp", [1, H]); din("up_bias_r", [128, NKH])
    din("b_con_r", [128, NF]); din("down_bias_r", [128, NF]); din("l0_gamma_r", [128, NF])
    din("wqkvaT", [D, 3 * D]); din("onehot", [1, 8])
    din("w1T_l1a", [16, H]); din("w2T_l1a", [H, D]); din("b1r_l1a", [128, NKH]); din("b2_l1a", [1, D])
    din("gamma1", [1, D]); din("beta1", [1, D]); din("gamma2_r", [128, NF]); din("beta2_r", [128, NF])
    din("routerT", [D, NE]); din("langT", [D, 4]); din("router_b", [1, NE]); din("lang_b", [1, 4])
    din("moe_w1_r", [NE, NKH, D, 128]); din("moe_w2", [NE, H, D])
    din("moe_b1_r", [128, NE * NKH]); din("moe_b2_r", [128, NE * NF])
    yT = nc.dram_tensor("yT", [D, T], F32, kind="ExternalOutput")

    fold_in = nc.dram_tensor("fold_in", [2, 128 * H], BF16)
    fold_out = nc.dram_tensor("fold_out", [2 * NCORES, 128 * H], BF16)
    att_in = nc.dram_tensor("att_in", [16, 128], F32)
    att_out = nc.dram_tensor("att_out", [16 * NCORES, 128], F32)
    brow_d = nc.dram_tensor("brow_d", [32, 128], F32)
    xspill = nc.dram_tensor("xspill", [D, T], F32)
    attn_fm = nc.dram_tensor("attn_fm", [D, T], F32)
    wnrows = nc.dram_tensor("wnrows", [NE, T], F32)

    BASE = 16512
    TOP = 229344

    def at(name, shape, dt, off):
        nbytes = int(np.prod(shape[1:])) * mybir.dt.size(dt)
        assert off % 32 == 0 and off + nbytes <= TOP, f"SBUF overflow {name} @{off}+{nbytes}"
        return nc.alloc_sbuf_tensor_at(name, list(shape), dt, offset=off)

    O_X = BASE
    O_ZB = O_X + 65536
    O_BIG = O_ZB + 32768
    O_SCA = O_BIG + 66752
    O_WNB = O_SCA + 8192
    O_WS1 = O_WNB + 8192
    O_WS3 = O_WS1 + 4096
    O_PH = O_WS3 + 4096
    O_SM = O_PH + 18432

    X = at("X", [128, NF, T], F32, O_X)
    ZB = at("ZB", [128, NF, T], BF16, O_ZB)
    HBL = at("HBL", [128, NKH, 512], BF16, O_ZB)
    HB = at("HB", [128, NKH, TH], BF16, O_BIG)
    SWB = at("SWB", [128, 2, H], F32, O_BIG)
    QTL = at("QTL", [128, NF, 2, 1024], BF16, O_BIG)
    X1 = at("X1", [128, 16, D], F32, O_BIG)
    A1C = at("A1C", [128, 16], F32, O_BIG + 65536)
    CARR = at("CARR", [128, 16], F32, O_BIG + 65600)
    GTHT = at("GTHT", [128, 128], F32, O_BIG + 65664)
    CC8 = at("CC8", [128, 8], F32, O_BIG + 66176)
    ATS = at("ATS", [16, 128], F32, O_BIG + 66208)
    SCA = at("SCA", [128, T], F32, O_SCA)
    WNB = at("WNB", [128, T], F32, O_WNB)
    FWS = at("FWS", [128, H], F32, O_SCA)
    WS1 = at("WS1", [128, 2, 8, 128], BF16, O_WS1)
    W1S = at("W1S", [16, 2, 128], BF16, O_WS1)
    WS3 = at("WS3", [128, 2, 1024], BF16, O_WS3)
    TMPF = at("TMPF", [128, H], F32, O_PH)
    HIT = at("HIT", [128, T], BF16, O_PH)
    LOT = at("LOT", [128, T], BF16, O_PH + 4096)
    HT1 = at("HT1", [128, 1024], F32, O_PH + 8192)
    HT2 = at("HT2", [128, 1024], F32, O_PH + 12288)
    STG = at("STG", [128, 32, 128], BF16, O_PH + 8192)
    EVS = at("EVS", [128, 512], F32, O_PH)
    SCO = at("SCO", [128, 1024], F32, O_PH)
    SCQ2 = at("SCQ2", [128, 1024], F32, O_PH + 4096)
    RLFR = at("RLFR", [8, T], F32, O_PH)
    RLFL = at("RLFL", [4, T], F32, O_PH + 8192)
    RLB = at("RLB", [128, 16, 16], F32, O_PH + 16384)
    WNM = at("WNM", [128, 16, 8], F32, O_PH + 17408)
    WNT8 = at("WNT8", [8, 128], F32, O_PH + 17920)
    WQT = at("WQT", [16, T], BF16, O_PH)
    B2BP = at("B2BP", [128, D], F32, O_PH + 4096)
    G1B = at("G1B", [128, D], F32, O_PH + 8192)
    QTM = at("QTM", [128, 16, 16], F32, O_PH + 12288)
    ST5 = at("ST5", [128, 1024], F32, O_PH + 13312)
    sm = [O_SM]

    def alloc(name, shape, dt):
        nbytes = int(np.prod(shape[1:])) * mybir.dt.size(dt)
        a = (sm[0] + 31) // 32 * 32
        sm[0] = a + nbytes
        return at(name, shape, dt, a)

    ONESB = alloc("ONESB", [128, 128], BF16)
    IDT = alloc("IDT", [128, 128], F32)
    COLS = alloc("COLS", [128, 64], F32)
    B1C = alloc("B1C", [128, NKH], F32)
    UBC = alloc("UBC", [128, NKH], F32)
    B2C = alloc("B2C", [128, NF], F32)
    DBC = alloc("DBC", [128, NF], F32)
    GAM = alloc("GAM", [128, NF], F32)
    MB1 = alloc("MB1", [128, NE * NKH], F32)
    MB2C = alloc("MB2C", [128, NE * NF], F32)
    B1CL = alloc("B1CL", [128, NKH], F32)
    G2C = alloc("G2C", [128, NF], F32)
    BE2C = alloc("BE2C", [128, NF], F32)
    RWT = alloc("RWT", [128, 8, 16], BF16)
    RBC = alloc("RBC", [16, 2], F32)
    OHB = alloc("OHB", [128, 8], F32)
    BNS = alloc("BNS", [128, 12], F32)
    BNA = alloc("BNA", [128, 4], F32)
    Q16 = alloc("Q16", [128, 2, 16], F32)
    assert sm[0] <= TOP, f"sbuf used {sm[0]}"

    PS = stack.enter_context(nc.psum_tensor("PS", [128, 8, 512], F32))
    PSW = bass.AP(PS, 0, [[8 * 512, 128], [1, 2048]])
    PSW2 = bass.AP(PS, 4 * 512, [[8 * 512, 128], [1, 2048]])

    for nm in ("qL", "qLW", "sW1a", "sW1b", "sW3a", "sW3b", "xcl", "stf",
               "mW1a", "mW1b", "mW3a", "mW3b", "PE", "DVE", "ACT", "PL", "CC", "ST"):
        cx.new(nm)

    t = nc.tensor; v = nc.vector; s = nc.scalar; g = nc.gpsimd; sy = nc.sync

    def dmaS(out, in_, sem="qL"):
        cx.inc(sy.dma_start(out=out, in_=in_), sem, 16)
        return (sem, cx.cnt[sem])

    def dmaW(out, in_, sem="qLW"):
        cx.inc(g.dma_start(out=out, in_=in_), sem, 16)
        return (sem, cx.cnt[sem])

    def dwait(eng, ld):
        cx.wait(eng, ld[0], ld[1])

    def dmaST(out, in_):
        cx.inc(sy.dma_start(out=out, in_=in_), "ST", 16)
        return cx.cnt["ST"]

    CONSTS = {}
    frozen = [False]

    def ccol(val):
        val = float(val)
        if val not in CONSTS:
            assert not frozen[0], f"const {val} after freeze"
            CONSTS[val] = len(CONSTS)
            assert len(CONSTS) <= 64
        return COLS[:, CONSTS[val]:CONSTS[val] + 1]

    for _v in (1.0 / D, 1e-5, float(D), 1e-12, 32.0, -1.0, 0.5, 0.1, 1.0, 0.0,
               1.0 / (2 * PI), 0.25, MAGIC, -MAGIC, -2 * PI, PI / 2):
        ccol(_v)
    frozen[0] = True

    bank_state = {}

    def bank_wait(eng, bank):
        bs = bank_state.get(bank)
        if bs is not None:
            cx.wait(eng, bs[0], bs[1])

    def bank_done(bank, sem="DVE"):
        bank_state[bank] = (sem, cx.cnt[sem])

    def halves(x, stage, half):
        sblk = 2 ** stage
        nb = H // (2 * sblk)
        return bass.AP(x.tensor, x.offset + half * sblk,
                       [list(x.ap[0]), [2 * sblk, nb], [1, sblk]])

    with nc.Block():
        dmaS(IDT[:], dr["ident"].ap())
        dmaS(GAM[:], dr["l0_gamma_r"].ap())
        dmaS(B2C[:], dr["b_con_r"].ap())
        dmaS(DBC[:], dr["down_bias_r"].ap())
        dmaS(UBC[:], dr["up_bias_r"].ap())
        dmaS(MB1[:], dr["moe_b1_r"].ap())
        dmaS(MB2C[:], dr["moe_b2_r"].ap())
        dmaS(OHB[:], bass.AP(dr["onehot"], 0, [[0, 128], [1, 8]]))
        dmaS(G2C[:], dr["gamma2_r"].ap())
        dmaS(BE2C[:], dr["beta2_r"].ap())
        dmaS(RBC[0:8, 0:1], bass.AP(dr["router_b"], 0, [[1, 8], [1, 1]]))
        dmaS(RBC[0:4, 1:2], bass.AP(dr["lang_b"], 0, [[1, 4], [1, 1]]))
        dmaS(B1CL[:], dr["b1r_l1a"].ap())
        x_ld = dmaS(X[:], dr["xT"].ap().rearrange("(f p) t -> p f t", p=128))
        dmaW(RWT[:, :, 0:8], dr["routerT"].ap().rearrange("(k p) e -> p k e", p=128))
        dmaW(RWT[:, :, 8:12], dr["langT"].ap().rearrange("(k p) e -> p k e", p=128))
        g.wait_ge(cx.sem["qLW"], cx.cnt["qLW"])
        g.memset(ONESB[:], 1.0)
        ii = g.memset(COLS[:], 0.0)
        cx.inc(ii, "PL"); cx.wait(g, "PL")
        last = None
        for val, idx in CONSTS.items():
            last = g.memset(COLS[:, idx:idx + 1], float(val))
        cx.inc(last, "PL")

    with nc.Block():
        for mat in range(3):
            arr = dr["bfu"] if mat != 1 else dr["bfd"]
            cx.wait(sy, "DVE")
            if mat == 0:
                ld_a = dmaS(FWS[:], dr["wexp_sl"].ap())
            elif mat == 1:
                ld_a = dmaS(FWS[:], dr["wconT_sl"].ap())
            else:
                ld_a = dmaS(FWS[:], bass.AP(dr["b_exp"], 0, [[0, 128], [1, H]]))
            for j in range(12):
                stage = j if mat != 1 else 11 - j
                cx.wait(sy, "DVE")
                dmaS(SWB[:, 0, :], bass.AP(arr, j * 2 * H, [[0, 128], [1, H]]))
                b1 = dmaS(SWB[:, 1, :], bass.AP(arr, j * 2 * H + H, [[0, 128], [1, H]]))
                cx.wait(v, "qL")
                w0 = SWB[:, 0, :]; w1 = SWB[:, 1, :]
                v.tensor_tensor(halves(TMPF[:], stage, 0), halves(FWS[:], stage, 1),
                                halves(w1, stage, 0), ALU.mult)
                ii = v.tensor_tensor(halves(TMPF[:], stage, 1), halves(FWS[:], stage, 0),
                                     halves(w1, stage, 1), ALU.mult)
                cx.inc(ii, "DVE"); cx.wait(v, "DVE")
                ii = v.tensor_tensor(FWS[:], FWS[:], w0, ALU.mult)
                cx.inc(ii, "DVE"); cx.wait(v, "DVE")
                ii = v.tensor_tensor(FWS[:], FWS[:], TMPF[:], ALU.add)
                cx.inc(ii, "DVE"); cx.wait(v, "DVE")
            if mat == 0:
                ii = v.tensor_copy(STG[:, 0:16, :], FWS[:, 0:2048])
                cx.inc(ii, "DVE"); cx.wait(v, "DVE")
                ii = v.tensor_copy(STG[:, 16:32, :], FWS[:, 2048:4096])
                cx.inc(ii, "DVE"); cx.wait(sy, "DVE")
                dmaS(bass.AP(fold_in, 0, [[H, 128], [1, H]]),
                     bass.AP(STG, 0, [[32 * 128, 128], [1, H]]))
            elif mat == 1:
                cx.wait(t, "DVE")
                for tt in range(NKH):
                    bank = tt % 2
                    bank_wait(t, bank)
                    ii = t.transpose(PS[:, bank, 0:128], FWS[:, 128 * tt:128 * (tt + 1)], IDT[:])
                    pe_v = cx.inc(ii, "PE")
                    cx.wait(v, "PE", pe_v)
                    ii = v.tensor_copy(STG[:, tt, :], PS[:, bank, 0:128])
                    cx.inc(ii, "DVE")
                    bank_done(bank)
                cx.wait(sy, "DVE")
                for qq in range(4):
                    dmaS(bass.AP(fold_in, 128 * H + qq * 8 * 128 * 128,
                                 [[128, 128], [128 * 128, 8], [1, 128]]),
                         STG[:, 8 * qq:8 * (qq + 1), :])
            else:
                cx.wait(sy, "DVE")
                dmaS(bass.AP(brow_d, 0, [[32 * 128, 1], [1, 32 * 128]]), FWS[0:1, :])

    with nc.Block():
        g.wait_ge(cx.sem["qL"], cx.cnt["qL"])
        ii = g.collective_compute("AllGather", ALU.bypass, replica_groups=[list(range(NCORES))],
                                  ins=[fold_in.ap().opt()], outs=[fold_out.ap().opt()])
        cx.inc(ii, "CC")
        cx.wait(sy, "qL")
        b1_ld = dmaS(GTHT[0:32, 0:128], brow_d.ap())
        dwait(t, b1_ld)
        bank_wait(t, 0)
        ii = t.transpose(PS[0:128, 0, 0:32], GTHT[0:32, 0:128], IDT[0:32, 0:32])
        pe_v = cx.inc(ii, "PE")
        cx.wait(v, "PE", pe_v)
        ii = v.tensor_copy(B1C[:], PS[:, 0, 0:32])
        cx.inc(ii, "DVE")
        bank_done(0)
        cx.wait(v, "DVE")
        ii = v.tensor_tensor(B1C[:], B1C[:], UBC[:], ALU.add)
        cx.inc(ii, "DVE")
        ii = v.tensor_tensor(B2C[:], B2C[:], DBC[:], ALU.add)
        cx.inc(ii, "DVE")
        cx.wait(v, "qL")

    def fm_ln(rms=False):
        pe_last = None
        for f in range(NF):
            xt = X[:, f, :]
            cx.wait(v, "PE")
            ii = v.tensor_copy(HIT[:], xt)
            cx.inc(ii, "DVE"); cx.wait(v, "DVE")
            ii = v.tensor_tensor(LOT[:], xt, HIT[:], ALU.subtract)
            cx.inc(ii, "DVE"); cx.wait(t, "DVE")
            for n in range(4):
                sl = slice(512 * n, 512 * (n + 1))
                if f == 0:
                    bank_wait(t, n)
                t.matmul(PS[:, n, :], ONESB[:, :], HIT[:, sl], start=(f == 0), stop=False)
                ii = t.matmul(PS[:, n, :], ONESB[:, :], LOT[:, sl], start=False, stop=(f == NF - 1))
            cx.inc(ii, "PE")
            ii = v.tensor_tensor(SCA[:], xt, xt, ALU.mult)
            cx.inc(ii, "DVE"); cx.wait(v, "DVE")
            cx.wait(v, "PE")
            ii = v.tensor_copy(HIT[:], SCA[:])
            cx.inc(ii, "DVE"); cx.wait(v, "DVE")
            ii = v.tensor_tensor(LOT[:], SCA[:], HIT[:], ALU.subtract)
            cx.inc(ii, "DVE"); cx.wait(t, "DVE")
            for n in range(4):
                sl = slice(512 * n, 512 * (n + 1))
                if f == 0:
                    bank_wait(t, 4 + n)
                t.matmul(PS[:, 4 + n, :], ONESB[:, :], HIT[:, sl], start=(f == 0), stop=False)
                ii = t.matmul(PS[:, 4 + n, :], ONESB[:, :], LOT[:, sl], start=False, stop=(f == NF - 1))
            pe_last = cx.inc(ii, "PE")
        cx.wait(v, "PE", pe_last)
        ii = v.tensor_scalar(WNB[:], PSW, ccol(1.0 / D), None, ALU.mult)
        cx.inc(ii, "DVE"); cx.wait(v, "DVE")
        ii = v.tensor_tensor(PSW, WNB[:], WNB[:], ALU.mult)
        cx.inc(ii, "DVE"); cx.wait(v, "DVE")
        ii = v.tensor_scalar(SCA[:], PSW2, ccol(1.0 / D), None, ALU.mult)
        cx.inc(ii, "DVE"); cx.wait(v, "DVE")
        ii = v.tensor_tensor(SCA[:], SCA[:], PSW, ALU.subtract)
        cx.inc(ii, "DVE"); cx.wait(s, "DVE")
        if rms:
            ii = s.activation(PSW2, SCA[:], AF.Sqrt, bias=ccol(0.0), scale=ccol(float(D)))
            cx.inc(ii, "ACT"); cx.wait(s, "ACT")
        ii = s.activation(PSW, SCA[:], AF.Sqrt, bias=ccol(1e-5), scale=ccol(1.0))
        cx.inc(ii, "ACT"); cx.wait(v, "ACT")
        ii = v.reciprocal(SCA[:], PSW)
        cx.inc(ii, "DVE"); cx.wait(v, "DVE")
        if rms:
            for hh, dst in ((0, HT1), (1, HT2)):
                hsl = slice(1024 * hh, 1024 * (hh + 1))
                pw = bass.AP(PS, 4 * 512 + 1024 * hh, [[8 * 512, 128], [1, 1024]])
                ii = v.tensor_tensor(dst[:], pw, SCA[:, hsl], ALU.mult)
                cx.inc(ii, "DVE"); cx.wait(v, "DVE")
                ii = v.tensor_scalar(dst[:], dst[:], ccol(1e-12), None, ALU.max)
                cx.inc(ii, "DVE"); cx.wait(v, "DVE")
                ii = v.reciprocal(dst[:], dst[:])
                cx.inc(ii, "DVE"); cx.wait(v, "DVE")
                ii = v.scalar_tensor_tensor(SCA[:, hsl], dst[:], ccol(32.0), SCA[:, hsl],
                                            ALU.mult, ALU.mult)
                cx.inc(ii, "DVE"); cx.wait(v, "DVE")
        ii = v.tensor_tensor(WNB[:], WNB[:], SCA[:], ALU.mult)
        cx.inc(ii, "DVE"); cx.wait(v, "DVE")
        ii = v.tensor_scalar(WNB[:], WNB[:], ccol(-1.0), None, ALU.mult)
        cx.inc(ii, "DVE"); cx.wait(v, "DVE")
        for f in range(NF):
            for hh, tmp in ((0, HT1), (1, HT2)):
                hsl = slice(1024 * hh, 1024 * (hh + 1))
                ii = v.tensor_tensor(tmp[:], X[:, f, hsl], SCA[:, hsl], ALU.mult)
                cx.inc(ii, "DVE"); cx.wait(v, "DVE")
                if rms:
                    ii = v.tensor_tensor(tmp[:], tmp[:], WNB[:, hsl], ALU.add)
                    cx.inc(ii, "DVE"); cx.wait(v, "DVE")
                    ii = v.tensor_scalar(ZB[:, f, hsl], tmp[:], GAM[:, f:f + 1], None, ALU.mult)
                    cx.inc(ii, "DVE"); cx.wait(v, "DVE")
                else:
                    ii = v.tensor_tensor(ZB[:, f, hsl], tmp[:], WNB[:, hsl], ALU.add)
                    cx.inc(ii, "DVE"); cx.wait(v, "DVE")
        for n in range(8):
            bank_done(n)
        cx.wait(t, "DVE")

    def stream_w1_slice(buf, m):
        ap = bass.AP(fold_out, 128 * m, [[H, 128], [2 * 128 * H, 8], [1, 128]])
        return dmaS(WS1[:, buf, :, :], ap, "sW1a" if buf == 0 else "sW1b")

    def stream_w2_chunk(buf, m):
        ap = bass.AP(fold_out, 128 * H + 128 * 128 * m, [[128, 128], [2 * 128 * H, 8], [1, 128]])
        return dmaS(WS3[:, buf, :], ap, "sW3a" if buf == 0 else "sW3b")

    def l0_ffn():
        fm_ln(rms=False)
        cx.wait(sy, "CC", 1)
        pe1_done = {}
        ld1 = {0: stream_w1_slice(0, 0)}
        for step in range(2 * NKH):
            half, m = step // NKH, step % NKH
            buf = step % 2
            if step + 1 < 2 * NKH:
                if step >= 1:
                    sy.wait_ge(cx.sem["PE"], pe1_done[step - 1])
                ld1[step + 1] = stream_w1_slice((step + 1) % 2, (step + 1) % NKH)
            dwait(t, ld1[step])
            for n in range(2):
                bank = (2 * step + n) % 4
                sl = slice(half * TH + 512 * n, half * TH + 512 * (n + 1))
                bank_wait(t, bank)
                for k in range(8):
                    ii = t.matmul(PS[:, bank, :], WS1[:, buf, k, :], ZB[:, k, sl],
                                  start=(k == 0), stop=(k == 7))
                pe_v = cx.inc(ii, "PE")
                cx.wait(s, "PE", pe_v)
                ii = s.activation(HB[:, m, 512 * n:512 * (n + 1)], PS[:, bank, :], AF.Gelu,
                                  bias=B1C[:, m:m + 1], scale=ccol(1.0))
                av = cx.inc(ii, "ACT")
                bank_state[bank] = ("ACT", av)
            pe1_done[step] = cx.cnt["PE"]
            if m == NKH - 1:
                for sw in range(2):
                    sl_t = slice(half * TH + 512 * sw, half * TH + 512 * (sw + 1))
                    pe2_done = {}
                    sy.wait_ge(cx.sem["PE"], cx.cnt["PE"])
                    ld2 = {0: stream_w2_chunk(0, 0)}
                    for m2 in range(NKH):
                        buf2 = m2 % 2
                        if m2 + 1 < NKH:
                            if m2 >= 1:
                                sy.wait_ge(cx.sem["PE"], pe2_done[m2 - 1])
                            ld2[m2 + 1] = stream_w2_chunk((m2 + 1) % 2, m2 + 1)
                        dwait(t, ld2[m2])
                        for dt in range(NF):
                            if m2 == 0:
                                bank_wait(t, dt)
                            ii = t.matmul(PS[:, dt, :], WS3[:, buf2, 128 * dt:128 * (dt + 1)],
                                          HB[:, m2, 512 * sw:512 * (sw + 1)],
                                          start=(m2 == 0), stop=(m2 == NKH - 1))
                        pe2_done[m2] = cx.inc(ii, "PE")
                    cx.wait(v, "PE", pe2_done[NKH - 1])
                    for dt in range(NF):
                        ii = v.tensor_scalar(EVS[:], PS[:, dt, :], B2C[:, dt:dt + 1],
                                             ccol(0.5), ALU.add, ALU.mult)
                        cx.inc(ii, "DVE"); cx.wait(v, "DVE")
                        ii = v.tensor_tensor(X[:, dt, sl_t], X[:, dt, sl_t], EVS[:], ALU.add)
                        cx.inc(ii, "DVE"); cx.wait(v, "DVE")
                        bank_done(dt)
                    cx.wait(t, "DVE")

    def l0_attn():
        fm_ln(rms=True)
        SCa = SCA[:, 0:1024]; SCs = SCA[:, 1024:2048]
        SCk = WNB[:, 0:1024]; SCp = WNB[:, 1024:2048]
        ii = v.memset(SCO[:], 1.0)
        cx.inc(ii, "DVE")
        pe_q = {}
        qk_ld = {0: dmaW(WS1[:, 0, :, :],
                         bass.AP(dr["wqkvaT"], D, [[3 * D, 128], [128 * 3 * D, 8], [1, 128]]),
                         "mW1a")}
        for half in range(2):
            hs = slice(half * TH, (half + 1) * TH)
            for f in range(NF):
                cx.wait(s, "DVE")
                for qi in range(3):
                    stp = (half * NF + f) * 3 + qi
                    buf = stp % 2
                    if stp + 1 < 48:
                        nstp = stp + 1
                        nf = (nstp % 24) // 3
                        nqi = nstp % 3
                        nbase = [D + 128 * nf, 2 * D + 128 * nf, 128 * nf][nqi]
                        if stp >= 1:
                            g.wait_ge(cx.sem["PE"], pe_q[stp - 1])
                        qk_ld[nstp] = dmaW(WS1[:, nstp % 2, :, :],
                                           bass.AP(dr["wqkvaT"], nbase,
                                                   [[3 * D, 128], [128 * 3 * D, 8], [1, 128]]),
                                           "mW1a" if nstp % 2 == 0 else "mW1b")
                    dwait(t, qk_ld[stp])
                    for n in range(2):
                        bank = 2 * qi + n
                        sl = slice(half * TH + 512 * n, half * TH + 512 * (n + 1))
                        bank_wait(t, bank)
                        for k in range(8):
                            ii = t.matmul(PS[:, bank, :], WS1[:, buf, k, :], ZB[:, k, sl],
                                          start=(k == 0), stop=(k == 7))
                        pe_v = cx.inc(ii, "PE")
                        cx.wait(s, "PE", pe_v)
                        dsl = slice(512 * n, 512 * (n + 1))
                        if qi == 0:
                            ii = s.copy(SCk[:, dsl], PS[:, bank, :])
                        elif qi == 1:
                            ii = s.activation(SCa[:, dsl], PS[:, bank, :], AF.Sigmoid)
                        else:
                            ii = s.copy(SCQ2[:, dsl], PS[:, bank, :])
                        av = cx.inc(ii, "ACT")
                        bank_state[bank] = ("ACT", av)
                    pe_q[stp] = cx.cnt["PE"]
                cx.wait(v, "ACT")
                if half == 0:
                    ii = v.tensor_tensor_scan(SCs, SCa, SCk, 0.0, ALU.mult, ALU.add)
                else:
                    ii = v.tensor_tensor_scan(SCs, SCa, SCk, A1C[:, 8 + f:8 + f + 1],
                                              ALU.mult, ALU.add)
                cx.inc(ii, "DVE"); cx.wait(v, "DVE")
                ii = v.tensor_tensor_scan(SCp, SCa, SCO[:], 1.0, ALU.mult, ALU.mult)
                cx.inc(ii, "DVE"); cx.wait(v, "DVE")
                if half == 0:
                    v.tensor_copy(A1C[:, f:f + 1], SCp[:, 1023:1024])
                    ii = v.tensor_copy(A1C[:, 8 + f:8 + f + 1], SCs[:, 1023:1024])
                    cx.inc(ii, "DVE"); cx.wait(v, "DVE")
                else:
                    v.tensor_scalar(CARR[:, f:f + 1], SCp[:, 1023:1024], A1C[:, f:f + 1],
                                    None, ALU.mult)
                    ii = v.tensor_copy(CARR[:, 8 + f:8 + f + 1], SCs[:, 1023:1024])
                    cx.inc(ii, "DVE"); cx.wait(v, "DVE")
                ii = v.tensor_tensor(QTL[:, f, half, :], SCQ2[:], SCp, ALU.mult)
                cx.inc(ii, "DVE"); cx.wait(v, "DVE")
                ii = v.tensor_tensor(SCk, SCQ2[:], SCs, ALU.mult)
                cx.inc(ii, "DVE"); cx.wait(v, "DVE")
                ii = v.tensor_tensor(X[:, f, hs], X[:, f, hs], SCk, ALU.add)
                cx.inc(ii, "DVE"); cx.wait(v, "DVE")
        cx.wait(t, "DVE")
        bank_wait(t, 0)
        ii = t.transpose(PS[0:16, 0, 0:128], CARR[:, 0:16], IDT[:])
        pe_v = cx.inc(ii, "PE")
        cx.wait(v, "PE", pe_v)
        ii = v.tensor_copy(ATS[:], PS[0:16, 0, 0:128])
        cx.inc(ii, "DVE")
        bank_done(0)
        cx.wait(sy, "DVE")
        at_st = dmaS(att_in.ap(), ATS[:])
        g.wait_ge(cx.sem["qL"], at_st[1])
        ii = g.collective_compute("AllGather", ALU.bypass, replica_groups=[list(range(NCORES))],
                                  ins=[att_in.ap().opt()], outs=[att_out.ap().opt()])
        cc_v = cx.inc(ii, "CC")
        cx.wait(sy, "CC", cc_v)
        gt_ld = dmaS(GTHT[:], att_out.ap())
        dwait(t, gt_ld)
        bank_wait(t, 1)
        ii = t.transpose(PS[:, 1, 0:128], GTHT[:], IDT[:])
        pe_v = cx.inc(ii, "PE")
        cx.wait(v, "PE", pe_v)
        ii = v.tensor_copy(GTHT[:], PS[:, 1, 0:128])
        cx.inc(ii, "DVE")
        bank_done(1)
        cx.wait(v, "DVE")
        for f in range(NF):
            ii = v.memset(CC8[:, 0:1], 0.0)
            cx.inc(ii, "DVE"); cx.wait(v, "DVE")
            for r in range(1, 8):
                if r == 4:
                    ii = v.memset(CC8[:, 4:5], 0.0)
                    cx.inc(ii, "DVE"); cx.wait(v, "DVE")
                    continue
                ii = v.scalar_tensor_tensor(CC8[:, r:r + 1], CC8[:, r - 1:r],
                                            GTHT[:, 16 * (r - 1) + f:16 * (r - 1) + f + 1],
                                            GTHT[:, 16 * (r - 1) + 8 + f:16 * (r - 1) + 8 + f + 1],
                                            ALU.mult, ALU.add)
                cx.inc(ii, "DVE"); cx.wait(v, "DVE")
            ii = v.tensor_tensor(CC8[:], CC8[:], OHB[:], ALU.mult)
            cx.inc(ii, "DVE"); cx.wait(v, "DVE")
            ii = v.reduce_sum(CARR[:, f:f + 1], CC8[:], axis=AXX)
            cx.inc(ii, "DVE"); cx.wait(v, "DVE")
            ii = v.tensor_scalar(CARR[:, 8 + f:8 + f + 1], A1C[:, f:f + 1], CARR[:, f:f + 1],
                                 None, ALU.mult)
            cx.inc(ii, "DVE"); cx.wait(v, "DVE")
        for f in range(NF):
            for half in range(2):
                hc = CARR[:, 8 * half + f:8 * half + f + 1]
                ii = v.tensor_scalar(SCk, QTL[:, f, half, :], hc, None, ALU.mult)
                cx.inc(ii, "DVE"); cx.wait(v, "DVE")
                hs = slice(half * TH, (half + 1) * TH)
                ii = v.tensor_tensor(X[:, f, hs], X[:, f, hs], SCk, ALU.add)
                cx.inc(ii, "DVE"); cx.wait(v, "DVE")

    def cos_ops(dst, src, scr):
        ii = v.tensor_scalar(scr, src, ccol(1.0 / (2 * PI)), ccol(0.25), ALU.mult, ALU.add)
        cx.inc(ii, "DVE"); cx.wait(v, "DVE")
        ii = v.tensor_scalar(scr, scr, ccol(MAGIC), ccol(-MAGIC), ALU.add, ALU.add)
        cx.inc(ii, "DVE"); cx.wait(v, "DVE")
        ii = v.scalar_tensor_tensor(scr, scr, ccol(-2 * PI), src, ALU.mult, ALU.add)
        cx.inc(ii, "DVE"); cx.wait(s, "DVE")
        ii = s.activation(dst, scr, AF.Sin, bias=ccol(PI / 2), scale=ccol(1.0))
        cx.inc(ii, "ACT"); cx.wait(v, "ACT")

    def moe():
        fm_ln(rms=False)
        for qi in range(2):
            nb = 8 if qi == 0 else 4
            dst = RLFR if qi == 0 else RLFL
            for n in range(4):
                bank = 4 * qi + n
                sl = slice(512 * n, 512 * (n + 1))
                bank_wait(t, bank)
                for k in range(8):
                    ii = t.matmul(PS[0:nb, bank, :], RWT[:, k, 8 * qi:8 * qi + nb], ZB[:, k, sl],
                                  start=(k == 0), stop=(k == 7))
                pe_v = cx.inc(ii, "PE")
                cx.wait(v, "PE", pe_v)
                ii = v.tensor_scalar(dst[0:nb, sl], PS[0:nb, bank, :], RBC[0:nb, qi:qi + 1],
                                     None, ALU.add)
                cx.inc(ii, "DVE")
                bank_done(bank)
        cx.wait(t, "DVE")
        for tt in range(16):
            bank = tt % 2
            bank2 = 2 + tt % 2
            bank_wait(t, bank); bank_wait(t, bank2)
            ii = t.transpose(PS[0:128, bank, 0:8], RLFR[:, 128 * tt:128 * (tt + 1)], IDT[0:8, 0:8])
            cx.inc(ii, "PE")
            ii2 = t.transpose(PS[0:128, bank2, 0:4], RLFL[:, 128 * tt:128 * (tt + 1)], IDT[0:4, 0:4])
            pe_v2 = cx.inc(ii2, "PE")
            cx.wait(v, "PE", pe_v2)
            v.tensor_copy(RLB[:, tt, 0:8], PS[:, bank, 0:8])
            ii = v.tensor_copy(RLB[:, tt, 8:12], PS[:, bank2, 0:4])
            cx.inc(ii, "DVE")
            bank_done(bank); bank_done(bank2)

        def batch(op, sem="DVE"):
            last = None
            for tt in range(16):
                last = op(tt)
            cx.inc(last, sem)
            cx.wait(v, sem); cx.wait(s, sem)

        cx.wait(v, "DVE")
        batch(lambda tt: v.reduce_max(RLB[:, tt, 12:13], RLB[:, tt, 8:12], axis=AXX))
        batch(lambda tt: v.tensor_scalar(RLB[:, tt, 8:12], RLB[:, tt, 8:12], RLB[:, tt, 12:13],
                                         None, ALU.subtract))
        batch(lambda tt: s.activation(RLB[:, tt, 8:12], RLB[:, tt, 8:12], AF.Exp), "ACT")
        batch(lambda tt: v.reduce_sum(RLB[:, tt, 12:13], RLB[:, tt, 8:12], axis=AXX))
        batch(lambda tt: v.reciprocal(RLB[:, tt, 12:13], RLB[:, tt, 12:13]))
        batch(lambda tt: v.tensor_scalar(RLB[:, tt, 8:12], RLB[:, tt, 8:12], RLB[:, tt, 12:13],
                                         None, ALU.mult))
        batch(lambda tt: v.tensor_tensor(RLB[:, tt, 12:15], RLB[:, tt, 8:11], RLB[:, tt, 9:12], ALU.add))
        batch(lambda tt: v.tensor_copy(RLB[:, tt, 15:16], RLB[:, tt, 11:12]))
        batch(lambda tt: v.scalar_tensor_tensor(RLB[:, tt, 0:4], RLB[:, tt, 12:16], ccol(0.1),
                                                RLB[:, tt, 0:4], ALU.mult, ALU.add))
        batch(lambda tt: v.scalar_tensor_tensor(RLB[:, tt, 4:8], RLB[:, tt, 12:16], ccol(0.1),
                                                RLB[:, tt, 4:8], ALU.mult, ALU.add))
        batch(lambda tt: v.reduce_max(RLB[:, tt, 12:13], RLB[:, tt, 0:8], axis=AXX))
        batch(lambda tt: v.tensor_scalar(RLB[:, tt, 0:8], RLB[:, tt, 0:8], RLB[:, tt, 12:13],
                                         None, ALU.subtract))
        batch(lambda tt: s.activation(RLB[:, tt, 0:8], RLB[:, tt, 0:8], AF.Exp), "ACT")
        batch(lambda tt: v.reduce_sum(RLB[:, tt, 12:13], RLB[:, tt, 0:8], axis=AXX))
        batch(lambda tt: v.reciprocal(RLB[:, tt, 12:13], RLB[:, tt, 12:13]))
        batch(lambda tt: v.tensor_scalar(RLB[:, tt, 0:8], RLB[:, tt, 0:8], RLB[:, tt, 12:13],
                                         None, ALU.mult))
        for tt in range(16):
            ii = v.max(BNS[:, 0:8], RLB[:, tt, 0:8])
            cx.inc(ii, "DVE"); cx.wait(v, "DVE")
            ii = v.tensor_scalar(WNM[:, tt, :], RLB[:, tt, 0:8], BNS[:, 1:2], None, ALU.is_ge)
            cx.inc(ii, "DVE"); cx.wait(v, "DVE")
        batch(lambda tt: v.tensor_tensor(WNM[:, tt, :], WNM[:, tt, :], RLB[:, tt, 0:8], ALU.mult))
        batch(lambda tt: v.reduce_sum(RLB[:, tt, 12:13], WNM[:, tt, :], axis=AXX))
        batch(lambda tt: v.reciprocal(RLB[:, tt, 12:13], RLB[:, tt, 12:13]))
        batch(lambda tt: v.tensor_scalar(WNM[:, tt, :], WNM[:, tt, :], RLB[:, tt, 12:13], ccol(0.5),
                                         ALU.mult, ALU.mult))
        cx.wait(t, "DVE")
        wn_st = 0
        for tt in range(16):
            bank = tt % 2
            bank_wait(t, bank)
            ii = t.transpose(PS[0:8, bank, 0:128], WNM[:, tt, :], IDT[:])
            pe_v = cx.inc(ii, "PE")
            cx.wait(v, "PE", pe_v)
            cx.wait(v, "qL")
            ii = v.tensor_copy(WNT8[:], PS[0:8, bank, 0:128])
            cx.inc(ii, "DVE")
            bank_done(bank)
            cx.wait(sy, "DVE")
            wn_st = dmaS(bass.AP(wnrows, 128 * tt, [[T, 8], [1, 128]]), WNT8[:])
        pe1_done = {}
        w1ld = {0: dmaW(WS1[:, 0, :, :],
                        dr["moe_w1_r"].ap()[0, 0].rearrange("(k p) m -> p k m", p=128), "mW1a")}
        for e in range(NE):
            dwait(sy, wn_st)
            cx.wait(sy, "DVE")
            wnb_ld = dmaS(WNB[:], bass.AP(wnrows, e * T, [[0, 128], [1, T]]))
            for half in range(2):
                ts0 = half * TH
                for m in range(NKH):
                    stp = (2 * e + half) * NKH + m
                    buf = stp % 2
                    if stp + 1 < 2 * NE * NKH:
                        ne_ = (stp + 1) // (2 * NKH)
                        nm_ = (stp + 1) % NKH
                        if stp >= 1:
                            g.wait_ge(cx.sem["PE"], pe1_done[stp - 1])
                        w1ld[stp + 1] = dmaW(WS1[:, (stp + 1) % 2, :, :],
                                             dr["moe_w1_r"].ap()[ne_, nm_].rearrange(
                                                 "(k p) m -> p k m", p=128),
                                             "mW1a" if (stp + 1) % 2 == 0 else "mW1b")
                    dwait(t, w1ld[stp])
                    for n in range(2):
                        bank = (2 * stp + n) % 4
                        sl = slice(ts0 + 512 * n, ts0 + 512 * (n + 1))
                        bank_wait(t, bank)
                        for k in range(8):
                            ii = t.matmul(PS[:, bank, :], WS1[:, buf, k, :], ZB[:, k, sl],
                                          start=(k == 0), stop=(k == 7))
                        pe_v = cx.inc(ii, "PE")
                        cx.wait(s, "PE", pe_v)
                        ii = s.activation(HB[:, m, 512 * n:512 * (n + 1)], PS[:, bank, :], AF.Gelu,
                                          bias=MB1[:, NKH * e + m:NKH * e + m + 1], scale=ccol(1.0))
                        av = cx.inc(ii, "ACT")
                        bank_state[bank] = ("ACT", av)
                    pe1_done[stp] = cx.cnt["PE"]
                for sw in range(2):
                    sl_t = slice(ts0 + 512 * sw, ts0 + 512 * (sw + 1))
                    pe2_done = {}
                    g.wait_ge(cx.sem["PE"], cx.cnt["PE"])
                    w2ld = {0: dmaW(WS3[:, 0, :], dr["moe_w2"].ap()[e, 0:128, :], "mW3a")}
                    for m2 in range(NKH):
                        buf2 = m2 % 2
                        if m2 + 1 < NKH:
                            if m2 >= 1:
                                g.wait_ge(cx.sem["PE"], pe2_done[m2 - 1])
                            w2ld[m2 + 1] = dmaW(WS3[:, (m2 + 1) % 2, :],
                                                dr["moe_w2"].ap()[e, 128 * (m2 + 1):128 * (m2 + 2), :],
                                                "mW3a" if (m2 + 1) % 2 == 0 else "mW3b")
                        dwait(t, w2ld[m2])
                        if m2 == 0:
                            dwait(t, wnb_ld)
                        for dt in range(NF):
                            if m2 == 0:
                                bank_wait(t, dt)
                            ii = t.matmul(PS[:, dt, :], WS3[:, buf2, 128 * dt:128 * (dt + 1)],
                                          HB[:, m2, 512 * sw:512 * (sw + 1)],
                                          start=(m2 == 0), stop=(m2 == NKH - 1))
                        pe2_done[m2] = cx.inc(ii, "PE")
                    cx.wait(v, "PE", pe2_done[NKH - 1])
                    for dt in range(NF):
                        ii = v.scalar_tensor_tensor(EVS[:], PS[:, dt, :],
                                                    MB2C[:, NF * e + dt:NF * e + dt + 1],
                                                    WNB[:, sl_t], ALU.add, ALU.mult)
                        cx.inc(ii, "DVE"); cx.wait(v, "DVE")
                        ii = v.tensor_tensor(X[:, dt, sl_t], X[:, dt, sl_t], EVS[:], ALU.add)
                        cx.inc(ii, "DVE"); cx.wait(v, "DVE")
                        bank_done(dt)
                    cx.wait(t, "DVE")

    def tm_ln_stats(srcv):
        v.bn_stats(BNS[:, 0:6], srcv[:, 0:512])
        ii = v.bn_stats(BNS[:, 6:12], srcv[:, 512:1024])
        cx.inc(ii, "DVE"); cx.wait(v, "DVE")
        ii = v.bn_aggr(BNA[:, 0:2], BNS[:, 0:12])
        cx.inc(ii, "DVE"); cx.wait(s, "DVE")
        ii = s.activation(BNA[:, 2:3], BNA[:, 1:2], AF.Sqrt, bias=ccol(1e-5), scale=ccol(1.0))
        cx.inc(ii, "ACT"); cx.wait(v, "ACT")
        ii = v.reciprocal(BNA[:, 2:3], BNA[:, 2:3])
        cx.inc(ii, "DVE"); cx.wait(v, "DVE")

    def l1_attn():
        TZ = SCA[:, 0:1024]; ZN = SCA[:, 1024:2048]
        CP1 = WNB[:, 0:1024]; CP2 = WNB[:, 1024:2048]
        xs = None
        for f in range(NF):
            xs = dmaST(bass.AP(xspill, f * 128 * T, [[T, 128], [1, T]]), X[:, f, :])
        cx.wait(sy, "ST", xs)
        bb_ld = dmaS(B2BP[:], bass.AP(dr["b2_l1a"], 0, [[0, 128], [1, D]]))
        g1_ld = dmaS(G1B[:], bass.AP(dr["gamma1"], 0, [[0, 128], [1, D]]))
        be_ld = dmaS(ST5[:], bass.AP(dr["beta1"], 0, [[0, 128], [1, D]]))
        dwait(v, be_ld)
        ii = v.tensor_tensor(B2BP[:], B2BP[:], ST5[:], ALU.add)
        cx.inc(ii, "DVE"); cx.wait(v, "DVE")
        v.tensor_copy(Q16[:, 0, :], G1B[:, 0:16])
        ii = v.tensor_copy(Q16[:, 1, :], ST5[:, 0:16])
        cx.inc(ii, "DVE"); cx.wait(v, "DVE")
        for tt in range(16):
            cx.wait(sy, "DVE")
            xc_ld = dmaS(bass.AP(WNB, 0, [[2048, 128], [128, 8], [1, 128]]),
                         bass.AP(xspill, 128 * tt, [[T, 128], [128 * T, 8], [1, 128]]), "xcl")
            dwait(t, xc_ld)
            for f in range(NF):
                bank = f % 2
                bank_wait(t, bank)
                ii = t.transpose(PS[:, bank, 0:128], CP1[:, 128 * f:128 * (f + 1)], IDT[:])
                pe_v = cx.inc(ii, "PE")
                cx.wait(v, "PE", pe_v)
                ii = v.tensor_copy(TZ[:, 128 * f:128 * (f + 1)], PS[:, bank, 0:128])
                cx.inc(ii, "DVE")
                bank_done(bank)
            cx.wait(v, "DVE")
            tm_ln_stats(TZ)
            ii = v.tensor_scalar(ZN, TZ, BNA[:, 0:1], BNA[:, 2:3], ALU.subtract, ALU.mult)
            cx.inc(ii, "DVE"); cx.wait(v, "DVE")
            cos_ops(CP1, ZN, CP2)
            src, dst = CP1, CP2
            for sh in (1, 2, 4, 8, 16, 32):
                v.tensor_copy(bass.AP(WNB, dst.offset, [[2048, 128], [64, 16], [1, sh]]),
                              bass.AP(WNB, src.offset, [[2048, 128], [64, 16], [1, sh]]))
                ii = v.tensor_tensor(
                    bass.AP(WNB, dst.offset + sh, [[2048, 128], [64, 16], [1, 64 - sh]]),
                    bass.AP(WNB, src.offset + sh, [[2048, 128], [64, 16], [1, 64 - sh]]),
                    bass.AP(WNB, src.offset, [[2048, 128], [64, 16], [1, 64 - sh]]),
                    ALU.mult)
                cx.inc(ii, "DVE"); cx.wait(v, "DVE")
                src, dst = dst, src
            ii = v.tensor_tensor(TZ, ZN, src, ALU.add)
            cx.inc(ii, "DVE"); cx.wait(v, "DVE")
            tm_ln_stats(TZ)
            ii = v.tensor_scalar(X1[:, tt, :], TZ, BNA[:, 0:1], BNA[:, 2:3], ALU.subtract, ALU.mult)
            cx.inc(ii, "DVE"); cx.wait(v, "DVE")
            ii = v.tensor_tensor(CP1[:, 0:16], X1[:, tt, 0:16], Q16[:, 0, :], ALU.mult)
            cx.inc(ii, "DVE"); cx.wait(v, "DVE")
            ii = v.tensor_tensor(CP1[:, 0:16], CP1[:, 0:16], Q16[:, 1, :], ALU.add)
            cx.inc(ii, "DVE"); cx.wait(v, "DVE")
            cos_ops(CP2[:, 0:16], CP1[:, 0:16], CP1[:, 16:32])
            src2, dst2 = CP2, CP1
            for sh in (1, 2, 4, 8):
                v.tensor_copy(dst2[:, 0:sh], src2[:, 0:sh])
                ii = v.tensor_tensor(dst2[:, sh:16], src2[:, sh:16], src2[:, 0:16 - sh], ALU.mult)
                cx.inc(ii, "DVE"); cx.wait(v, "DVE")
                src2, dst2 = dst2, src2
            ii = v.tensor_copy(QTM[:, tt, :], src2[:, 0:16])
            cx.inc(ii, "DVE"); cx.wait(v, "DVE")
        cx.wait(t, "DVE")
        for tt in range(16):
            bank = tt % 2
            bank_wait(t, bank)
            ii = t.transpose(PS[0:16, bank, 0:128], QTM[:, tt, :], IDT[:])
            pe_v = cx.inc(ii, "PE")
            cx.wait(v, "PE", pe_v)
            ii = v.tensor_copy(WQT[:, 128 * tt:128 * (tt + 1)], PS[0:16, bank, 0:128])
            cx.inc(ii, "DVE")
            bank_done(bank)
        w1_ld = {0: dmaW(W1S[:, 0, :], dr["w1T_l1a"].ap()[:, 0:128], "mW1a")}
        pe1v = {}
        for q in range(4):
            qs = slice(512 * q, 512 * (q + 1))
            cx.wait(t, "DVE")
            for m in range(NKH):
                stp = q * NKH + m
                buf = stp % 2
                if stp + 1 < 4 * NKH:
                    nm_ = (stp + 1) % NKH
                    if stp >= 1:
                        g.wait_ge(cx.sem["PE"], pe1v[stp - 1])
                    w1_ld[stp + 1] = dmaW(W1S[:, (stp + 1) % 2, :],
                                          dr["w1T_l1a"].ap()[:, 128 * nm_:128 * (nm_ + 1)],
                                          "mW1a" if (stp + 1) % 2 == 0 else "mW1b")
                dwait(t, w1_ld[stp])
                bank = m % 4
                bank_wait(t, bank)
                ii = t.matmul(PS[:, bank, :], W1S[:, buf, :], WQT[:, qs], start=True, stop=True)
                pe_v = cx.inc(ii, "PE")
                pe1v[stp] = pe_v
                cx.wait(s, "PE", pe_v)
                ii = s.activation(HBL[:, m, 0:512], PS[:, bank, :], AF.Relu,
                                  bias=B1CL[:, m:m + 1], scale=ccol(1.0))
                av = cx.inc(ii, "ACT")
                bank_state[bank] = ("ACT", av)
            pe2_done = {}
            g.wait_ge(cx.sem["PE"], cx.cnt["PE"])
            w2ld = {0: dmaW(WS3[:, 0, :], dr["w2T_l1a"].ap()[0:128, :], "mW3a")}
            for m2 in range(NKH):
                buf2 = m2 % 2
                if m2 + 1 < NKH:
                    if m2 >= 1:
                        g.wait_ge(cx.sem["PE"], pe2_done[m2 - 1])
                    w2ld[m2 + 1] = dmaW(WS3[:, (m2 + 1) % 2, :],
                                        dr["w2T_l1a"].ap()[128 * (m2 + 1):128 * (m2 + 2), :],
                                        "mW3a" if (m2 + 1) % 2 == 0 else "mW3b")
                dwait(t, w2ld[m2])
                for ttq in range(4):
                    for n in range(2):
                        bank = 2 * ttq + n
                        if m2 == 0:
                            bank_wait(t, bank)
                        ii = t.matmul(PS[:, bank, :], HBL[:, m2, 128 * ttq:128 * (ttq + 1)],
                                      WS3[:, buf2, 512 * n:512 * (n + 1)],
                                      start=(m2 == 0), stop=(m2 == NKH - 1))
                pe2_done[m2] = cx.inc(ii, "PE")
            cx.wait(v, "PE", pe2_done[NKH - 1])
            for ttq in range(4):
                tt = 4 * q + ttq
                ii = v.tensor_tensor(TZ, X1[:, tt, :], G1B[:], ALU.mult)
                cx.inc(ii, "DVE"); cx.wait(v, "DVE")
                ii = v.tensor_tensor(TZ, TZ, B2BP[:], ALU.add)
                cx.inc(ii, "DVE"); cx.wait(v, "DVE")
                for n in range(2):
                    bank = 2 * ttq + n
                    dsl = slice(512 * n, 512 * (n + 1))
                    ii = v.tensor_tensor(TZ[:, dsl], TZ[:, dsl], PS[:, bank, :], ALU.add)
                cx.inc(ii, "DVE"); cx.wait(v, "DVE")
                bank_done(2 * ttq); bank_done(2 * ttq + 1)
                tm_ln_stats(TZ)
                ii = v.tensor_scalar(ZN, TZ, BNA[:, 0:1], BNA[:, 2:3], ALU.subtract, ALU.mult)
                cx.inc(ii, "DVE"); cx.wait(t, "DVE")
                cx.wait(v, "stf")
                for f in range(NF):
                    bank = f % 2
                    bank_wait(t, bank)
                    ii = t.transpose(PS[:, bank, 0:128], ZN[:, 128 * f:128 * (f + 1)], IDT[:])
                    pe_v = cx.inc(ii, "PE")
                    cx.wait(v, "PE", pe_v)
                    ii = v.tensor_scalar(ST5[:, 128 * f:128 * (f + 1)], PS[:, bank, 0:128],
                                         G2C[:, f:f + 1], BE2C[:, f:f + 1], ALU.mult, ALU.add)
                    cx.inc(ii, "DVE")
                    bank_done(bank)
                cx.wait(sy, "DVE")
                dmaS(bass.AP(attn_fm, 128 * tt, [[T, 128], [128 * T, 8], [1, 128]]),
                     bass.AP(ST5, 0, [[1024, 128], [128, 8], [1, 128]]), "stf")
        cx.wait(sy, "stf")
        for f in range(NF):
            if f > 0:
                cx.wait(sy, "xcl")
                cx.wait(sy, "DVE")
            dmaS(X[:, f, :], bass.AP(xspill, f * 128 * T, [[T, 128], [1, T]]), "xcl")
            a_ld = dmaS(SCA[:], bass.AP(attn_fm, f * 128 * T, [[T, 128], [1, T]]), "xcl")
            dwait(v, a_ld)
            ii = v.tensor_tensor(X[:, f, :], X[:, f, :], SCA[:], ALU.add)
            cx.inc(ii, "DVE"); cx.wait(v, "DVE")

    if bp >= 1:
        with nc.Block():
            l0_ffn()
    if bp >= 2:
        with nc.Block():
            l0_attn()
    if bp >= 3:
        with nc.Block():
            l0_ffn()
    if bp >= 4:
        with nc.Block():
            moe()
    if bp >= 5:
        with nc.Block():
            l1_attn()
    if bp >= 6:
        with nc.Block():
            moe()

    with nc.Block():
        stv = None
        for f in range(NF):
            stv = dmaST(bass.AP(yT, f * 128 * T, [[T, 128], [1, T]]), X[:, f, :])
        g.wait_ge(cx.sem["ST"], stv)

    stack.close()
    return nc


_CACHE = {}


def kernel(**inputs):
    per_core = prep_inputs(inputs)
    key = BUILD_PHASES
    if key not in _CACHE:
        _CACHE[key] = build_program()
    nc = _CACHE[key]
    res = run_bass_kernel_spmd(nc, per_core, core_ids=list(range(NCORES)))
    out = np.empty((2, 4 * T, D), np.float32)
    for c in range(NCORES):
        b, q = c // 4, c % 4
        out[b, q * T:(q + 1) * T, :] = res.results[c]["yT"].T
    return out


# version 02 (final)
# speedup vs baseline: 1.2906x; 1.2906x over previous
import contextlib
import numpy as np
import ml_dtypes
import concourse.bass as bass
import concourse.mybir as mybir
from concourse.bass_utils import run_bass_kernel_spmd

F32 = mybir.dt.float32
BF16 = mybir.dt.bfloat16
AF = mybir.ActivationFunctionType
ALU = mybir.AluOpType
AXX = mybir.AxisListType.X

NCORES = 8
D = 1024
T = 2048
TH = T // 2
H = 4096
NF = D // 128
NKH = H // 128
NE = 8
PI = float(np.pi)
MAGIC = 12582912.0

BUILD_PHASES = 99


def _stage_vectors(weights, dim, transpose=False):
    out = []
    f = np.arange(dim)
    for stage in range(weights.shape[0]):
        s = 2 ** stage
        bit = (f // s) % 2
        k = (f // (2 * s)) * s + (f % s)
        w = weights[stage]
        if not transpose:
            W0 = np.where(bit == 0, w[k, 0, 0], w[k, 1, 1])
            W1 = np.where(bit == 0, w[k, 0, 1], w[k, 1, 0])
        else:
            W0 = np.where(bit == 0, w[k, 0, 0], w[k, 1, 1])
            W1 = np.where(bit == 0, w[k, 1, 0], w[k, 0, 1])
        out.append((W0.astype(np.float32), W1.astype(np.float32)))
    return out


def prep_inputs(inp):
    c32 = np.ascontiguousarray
    sh = {}
    svu = _stage_vectors(np.asarray(inp["L0_ffn_up_weights"], np.float32), H, False)
    svd = _stage_vectors(np.asarray(inp["L0_ffn_down_weights"], np.float32), H, True)
    sh["bfu"] = c32(np.stack([np.stack(p) for p in svu]))
    sh["bfd"] = c32(np.stack([np.stack(svd[s]) for s in reversed(range(12))]))
    sh["ident"] = np.eye(128, dtype=np.float32)
    sh["b_exp"] = c32(np.asarray(inp["L0_ffn_b_exp"], np.float32).reshape(1, H))
    sh["up_bias_r"] = c32(np.asarray(inp["L0_ffn_up_bias"], np.float32).reshape(NKH, 128).T)
    sh["b_con_r"] = c32(np.asarray(inp["L0_ffn_b_con"], np.float32).reshape(NF, 128).T)
    sh["down_bias_r"] = c32(np.asarray(inp["L0_ffn_down_bias"], np.float32).reshape(NF, 128).T)
    sh["l0_gamma_r"] = c32(np.asarray(inp["L0_attn_gamma"], np.float32).reshape(NF, 128).T)
    sh["wqkvaT"] = c32(np.asarray(inp["L0_attn_w_qkva"], np.float32).T)
    sh["w1T_l1a"] = c32(np.asarray(inp["L1_attn_w1"], np.float32).T)
    sh["w2T_l1a"] = c32(np.asarray(inp["L1_attn_w2"], np.float32).T)
    sh["b1r_l1a"] = c32(np.asarray(inp["L1_attn_b1"], np.float32).reshape(NKH, 128).T)
    sh["b2_l1a"] = c32(np.asarray(inp["L1_attn_b2"], np.float32).reshape(1, D))
    sh["gamma1"] = c32(np.asarray(inp["L1_attn_gamma1"], np.float32).reshape(1, D))
    sh["beta1"] = c32(np.asarray(inp["L1_attn_beta1"], np.float32).reshape(1, D))
    sh["gamma2_r"] = c32(np.asarray(inp["L1_attn_gamma2"], np.float32).reshape(NF, 128).T)
    sh["beta2_r"] = c32(np.asarray(inp["L1_attn_beta2"], np.float32).reshape(NF, 128).T)
    sh["routerT"] = c32(np.asarray(inp["L1_ffn_router_w"], np.float32).T)
    sh["langT"] = c32(np.asarray(inp["L1_ffn_lang_w"], np.float32).T)
    sh["router_b"] = c32(np.asarray(inp["L1_ffn_router_b"], np.float32).reshape(1, NE))
    sh["lang_b"] = c32(np.asarray(inp["L1_ffn_lang_b"], np.float32).reshape(1, 4))
    w1 = np.asarray(inp["L1_ffn_w1"], np.float32)
    sh["moe_w1_r"] = c32(w1.reshape(NE, D, NKH, 128).transpose(0, 2, 1, 3))
    sh["moe_w2"] = c32(np.asarray(inp["L1_ffn_w2"], np.float32))
    sh["moe_b1_r"] = c32(np.asarray(inp["L1_ffn_b1"], np.float32).reshape(NE, NKH, 128)
                         .transpose(2, 0, 1).reshape(128, NE * NKH))
    sh["moe_b2_r"] = c32(np.asarray(inp["L1_ffn_b2"], np.float32).reshape(NE, NF, 128)
                         .transpose(2, 0, 1).reshape(128, NE * NF))
    wexp = np.asarray(inp["L0_ffn_W_exp"], np.float32)
    wconT = c32(np.asarray(inp["L0_ffn_W_con"], np.float32).T)
    x = np.asarray(inp["L0_attn_x"], np.float32)
    per_core = []
    for c in range(NCORES):
        b, q = c // 4, c % 4
        m = dict(sh)
        m["xT"] = c32(x[b, q * T:(q + 1) * T, :].T)
        m["wexp_sl"] = c32(wexp[128 * c:128 * (c + 1), :])
        m["wconT_sl"] = c32(wconT[128 * c:128 * (c + 1), :])
        oh = np.zeros((1, 8), np.float32)
        oh[0, c] = 1.0
        m["onehot"] = oh
        per_core.append(m)
    return per_core


class Ctx:
    def __init__(self, nc, stack):
        self.nc = nc
        self.stack = stack
        self.sem = {}
        self.cnt = {}

    def new(self, name):
        self.sem[name] = self.stack.enter_context(self.nc.semaphore(name))
        self.cnt[name] = 0

    def inc(self, instr, name, n=1):
        instr.then_inc(self.sem[name], n)
        self.cnt[name] += n
        return self.cnt[name]

    def wait(self, eng, name, val=None):
        eng.wait_ge(self.sem[name], self.cnt[name] if val is None else val)


def build_program(build_phases=None):
    bp = BUILD_PHASES if build_phases is None else build_phases
    nc = bass.Bass("TRN2", target_bir_lowering=False)
    stack = contextlib.ExitStack()
    cx = Ctx(nc, stack)
    dr = {}

    def din(name, shape):
        dr[name] = nc.dram_tensor(name, list(shape), F32, kind="ExternalInput")

    din("xT", [D, T]); din("wexp_sl", [128, H]); din("wconT_sl", [128, H])
    din("bfu", [12, 2, H]); din("bfd", [12, 2, H]); din("ident", [128, 128])
    din("b_exp", [1, H]); din("up_bias_r", [128, NKH])
    din("b_con_r", [128, NF]); din("down_bias_r", [128, NF]); din("l0_gamma_r", [128, NF])
    din("wqkvaT", [D, 3 * D]); din("onehot", [1, 8])
    din("w1T_l1a", [16, H]); din("w2T_l1a", [H, D]); din("b1r_l1a", [128, NKH]); din("b2_l1a", [1, D])
    din("gamma1", [1, D]); din("beta1", [1, D]); din("gamma2_r", [128, NF]); din("beta2_r", [128, NF])
    din("routerT", [D, NE]); din("langT", [D, 4]); din("router_b", [1, NE]); din("lang_b", [1, 4])
    din("moe_w1_r", [NE, NKH, D, 128]); din("moe_w2", [NE, H, D])
    din("moe_b1_r", [128, NE * NKH]); din("moe_b2_r", [128, NE * NF])
    yT = nc.dram_tensor("yT", [D, T], F32, kind="ExternalOutput")

    fold_in = nc.dram_tensor("fold_in", [2, 128 * H], BF16)
    fold_out = nc.dram_tensor("fold_out", [2 * NCORES, 128 * H], BF16)
    att_in = nc.dram_tensor("att_in", [16, 128], F32)
    att_out = nc.dram_tensor("att_out", [16 * NCORES, 128], F32)
    brow_d = nc.dram_tensor("brow_d", [32, 128], F32)
    xspill = nc.dram_tensor("xspill", [D, T], F32)
    attn_fm = nc.dram_tensor("attn_fm", [D, T], F32)
    wnrows = nc.dram_tensor("wnrows", [NE, T], F32)

    BASE = 16512
    TOP = 229344

    def at(name, shape, dt, off):
        nbytes = int(np.prod(shape[1:])) * mybir.dt.size(dt)
        assert off % 32 == 0 and off + nbytes <= TOP, f"SBUF overflow {name} @{off}+{nbytes}"
        return nc.alloc_sbuf_tensor_at(name, list(shape), dt, offset=off)

    O_X = BASE
    O_ZB = O_X + 65536
    O_BIG = O_ZB + 32768
    O_SCA = O_BIG + 66752
    O_WNB = O_SCA + 8192
    O_WS1 = O_WNB + 8192
    O_WS3 = O_WS1 + 6144
    O_PH = O_WS3 + 6144
    O_SM = O_PH + 12288

    X = at("X", [128, NF, T], F32, O_X)
    ZB = at("ZB", [128, NF, T], BF16, O_ZB)
    HBL = at("HBL", [128, NKH, 512], BF16, O_ZB)
    HB = at("HB", [128, NKH, TH], BF16, O_BIG)
    SWB = at("SWB", [128, 2, H], F32, O_BIG)
    QTL = at("QTL", [128, NF, 2, 1024], BF16, O_BIG)
    X1 = at("X1", [128, 16, D], F32, O_BIG)
    A1C = at("A1C", [128, 16], F32, O_BIG + 65536)
    CARR = at("CARR", [128, 16], F32, O_BIG + 65600)
    GTHT = at("GTHT", [128, 128], F32, O_BIG + 65664)
    CC8 = at("CC8", [128, 8], F32, O_BIG + 66176)
    ATS = at("ATS", [16, 128], F32, O_BIG + 66208)
    SCA = at("SCA", [128, T], F32, O_SCA)
    WNB = at("WNB", [128, T], F32, O_WNB)
    ST5 = at("ST5", [128, 1024], F32, O_WNB)
    FWS = at("FWS", [128, H], F32, O_SCA)
    WS1 = at("WS1", [128, 3, 8, 128], BF16, O_WS1)
    W1S = at("W1S", [16, 2, 128], BF16, O_WS1)
    WS3 = at("WS3", [128, 3, 1024], BF16, O_WS3)
    TMPF = at("TMPF", [128, H], F32, O_WS1)
    HIT = at("HIT", [128, T], BF16, O_WS1)
    LOT = at("LOT", [128, T], BF16, O_WS3)
    HT1 = at("HT1", [128, 1024], F32, O_PH)
    HT2 = at("HT2", [128, 1024], F32, O_PH + 4096)
    STG = at("STG", [128, 32, 128], BF16, O_PH)
    EVS = at("EVS", [128, 4, 512], F32, O_PH)
    SCO = at("SCO", [128, 1024], F32, O_PH)
    SCQ2 = at("SCQ2", [128, 1024], F32, O_PH + 4096)
    RSTG = at("RSTG", [16, 128], F32, O_PH + 8192)
    RLB = at("RLB", [128, 16, 16], F32, O_PH + 8704)
    WNM = at("WNM", [128, 16, 8], F32, O_PH + 9728)
    WNT8 = at("WNT8", [8, 128], F32, O_PH + 10240)
    WQT = at("WQT", [16, T], BF16, O_PH)
    B2BP = at("B2BP", [128, D], F32, O_PH + 4096)
    G1B = at("G1B", [128, D], F32, O_PH + 8192)
    sm = [O_SM]

    def alloc(name, shape, dt):
        nbytes = int(np.prod(shape[1:])) * mybir.dt.size(dt)
        a = (sm[0] + 31) // 32 * 32
        sm[0] = a + nbytes
        return at(name, shape, dt, a)

    ONESB = alloc("ONESB", [128, 128], BF16)
    IDT = alloc("IDT", [128, 128], F32)
    COLS = alloc("COLS", [128, 64], F32)
    B1C = alloc("B1C", [128, NKH], F32)
    UBC = alloc("UBC", [128, NKH], F32)
    B2C = alloc("B2C", [128, NF], F32)
    DBC = alloc("DBC", [128, NF], F32)
    GAM = alloc("GAM", [128, NF], F32)
    MB1 = alloc("MB1", [128, NE * NKH], F32)
    MB2C = alloc("MB2C", [128, NE * NF], F32)
    B1CL = alloc("B1CL", [128, NKH], F32)
    G2C = alloc("G2C", [128, NF], F32)
    BE2C = alloc("BE2C", [128, NF], F32)
    RWT = alloc("RWT", [128, 8, 16], BF16)
    RBC = alloc("RBC", [16, 2], F32)
    OHB = alloc("OHB", [128, 8], F32)
    BNS = alloc("BNS", [128, 12], F32)
    BNA = alloc("BNA", [128, 4], F32)
    Q16 = alloc("Q16", [128, 2, 16], F32)
    QTM = alloc("QTM", [128, 16, 16], F32)
    assert sm[0] <= TOP, f"sbuf used {sm[0]}"

    PS = stack.enter_context(nc.psum_tensor("PS", [128, 8, 512], F32))
    PSW = bass.AP(PS, 0, [[8 * 512, 128], [1, 2048]])
    PSW2 = bass.AP(PS, 4 * 512, [[8 * 512, 128], [1, 2048]])

    for nm in ("qL", "qLW", "sW1a", "sW1b", "sW1c", "sW3a", "sW3b", "sW3c", "xcl", "stf",
               "mW1a", "mW1b", "mW1c", "mW3a", "mW3b", "mW3c", "PE", "DVE", "ACT", "PL", "CC", "ST"):
        cx.new(nm)

    t = nc.tensor; v = nc.vector; s = nc.scalar; g = nc.gpsimd; sy = nc.sync

    def dmaS(out, in_, sem="qL"):
        cx.inc(sy.dma_start(out=out, in_=in_), sem, 16)
        return (sem, cx.cnt[sem])

    def dmaW(out, in_, sem="qLW"):
        cx.inc(g.dma_start(out=out, in_=in_), sem, 16)
        return (sem, cx.cnt[sem])

    def dwait(eng, ld):
        cx.wait(eng, ld[0], ld[1])

    def dmaST(out, in_):
        cx.inc(sy.dma_start(out=out, in_=in_), "ST", 16)
        return cx.cnt["ST"]

    CONSTS = {}
    frozen = [False]

    def ccol(val):
        val = float(val)
        if val not in CONSTS:
            assert not frozen[0], f"const {val} after freeze"
            CONSTS[val] = len(CONSTS)
            assert len(CONSTS) <= 64
        return COLS[:, CONSTS[val]:CONSTS[val] + 1]

    for _v in (1.0 / D, 1e-5, float(D), 1e-12, 32.0, -1.0, 0.5, 0.1, 1.0, 0.0,
               1.0 / (2 * PI), 0.25, MAGIC, -MAGIC, -2 * PI, PI / 2):
        ccol(_v)
    frozen[0] = True

    bank_state = {}

    def bank_wait(eng, bank):
        bs = bank_state.get(bank)
        if bs is not None:
            cx.wait(eng, bs[0], bs[1])

    def bank_done(bank, sem="DVE"):
        bank_state[bank] = (sem, cx.cnt[sem])

    def halves(x, stage, half):
        sblk = 2 ** stage
        nb = H // (2 * sblk)
        return bass.AP(x.tensor, x.offset + half * sblk,
                       [list(x.ap[0]), [2 * sblk, nb], [1, sblk]])

    with nc.Block():
        dmaS(IDT[:], dr["ident"].ap())
        dmaS(GAM[:], dr["l0_gamma_r"].ap())
        dmaS(B2C[:], dr["b_con_r"].ap())
        dmaS(DBC[:], dr["down_bias_r"].ap())
        dmaS(UBC[:], dr["up_bias_r"].ap())
        dmaS(MB1[:], dr["moe_b1_r"].ap())
        dmaS(MB2C[:], dr["moe_b2_r"].ap())
        dmaS(OHB[:], bass.AP(dr["onehot"], 0, [[0, 128], [1, 8]]))
        dmaS(G2C[:], dr["gamma2_r"].ap())
        dmaS(BE2C[:], dr["beta2_r"].ap())
        dmaS(RBC[0:8, 0:1], bass.AP(dr["router_b"], 0, [[1, 8], [1, 1]]))
        dmaS(RBC[0:4, 1:2], bass.AP(dr["lang_b"], 0, [[1, 4], [1, 1]]))
        dmaS(B1CL[:], dr["b1r_l1a"].ap())
        x_ld = dmaS(X[:], dr["xT"].ap().rearrange("(f p) t -> p f t", p=128))
        dmaW(RWT[:, :, 0:8], dr["routerT"].ap().rearrange("(k p) e -> p k e", p=128))
        dmaW(RWT[:, :, 8:12], dr["langT"].ap().rearrange("(k p) e -> p k e", p=128))
        g.wait_ge(cx.sem["qLW"], cx.cnt["qLW"])
        g.memset(ONESB[:], 1.0)
        ii = g.memset(COLS[:], 0.0)
        cx.inc(ii, "PL"); cx.wait(g, "PL")
        last = None
        for val, idx in CONSTS.items():
            last = g.memset(COLS[:, idx:idx + 1], float(val))
        cx.inc(last, "PL")

    with nc.Block():
        for mat in range(3):
            arr = dr["bfu"] if mat != 1 else dr["bfd"]
            cx.wait(sy, "DVE")
            if mat == 0:
                ld_a = dmaS(FWS[:], dr["wexp_sl"].ap())
            elif mat == 1:
                ld_a = dmaS(FWS[:], dr["wconT_sl"].ap())
            else:
                ld_a = dmaS(FWS[:], bass.AP(dr["b_exp"], 0, [[0, 128], [1, H]]))
            for j in range(12):
                stage = j if mat != 1 else 11 - j
                cx.wait(sy, "DVE")
                dmaS(SWB[:, 0, :], bass.AP(arr, j * 2 * H, [[0, 128], [1, H]]))
                b1 = dmaS(SWB[:, 1, :], bass.AP(arr, j * 2 * H + H, [[0, 128], [1, H]]))
                cx.wait(v, "qL")
                w0 = SWB[:, 0, :]; w1 = SWB[:, 1, :]
                v.tensor_tensor(halves(TMPF[:], stage, 0), halves(FWS[:], stage, 1),
                                halves(w1, stage, 0), ALU.mult)
                ii = v.tensor_tensor(halves(TMPF[:], stage, 1), halves(FWS[:], stage, 0),
                                     halves(w1, stage, 1), ALU.mult)
                cx.inc(ii, "DVE"); cx.wait(v, "DVE")
                ii = v.tensor_tensor(FWS[:], FWS[:], w0, ALU.mult)
                cx.inc(ii, "DVE"); cx.wait(v, "DVE")
                ii = v.tensor_tensor(FWS[:], FWS[:], TMPF[:], ALU.add)
                cx.inc(ii, "DVE"); cx.wait(v, "DVE")
            if mat == 0:
                ii = v.tensor_copy(STG[:, 0:16, :], FWS[:, 0:2048])
                cx.inc(ii, "DVE"); cx.wait(v, "DVE")
                ii = v.tensor_copy(STG[:, 16:32, :], FWS[:, 2048:4096])
                cx.inc(ii, "DVE"); cx.wait(sy, "DVE")
                dmaS(bass.AP(fold_in, 0, [[H, 128], [1, H]]),
                     bass.AP(STG, 0, [[32 * 128, 128], [1, H]]))
            elif mat == 1:
                cx.wait(t, "DVE")
                for tt in range(NKH):
                    bank = tt % 2
                    bank_wait(t, bank)
                    ii = t.transpose(PS[:, bank, 0:128], FWS[:, 128 * tt:128 * (tt + 1)], IDT[:])
                    pe_v = cx.inc(ii, "PE")
                    cx.wait(v, "PE", pe_v)
                    ii = v.tensor_copy(STG[:, tt, :], PS[:, bank, 0:128])
                    cx.inc(ii, "DVE")
                    bank_done(bank)
                cx.wait(sy, "DVE")
                for qq in range(4):
                    dmaS(bass.AP(fold_in, 128 * H + qq * 8 * 128 * 128,
                                 [[128, 128], [128 * 128, 8], [1, 128]]),
                         STG[:, 8 * qq:8 * (qq + 1), :])
            else:
                cx.wait(sy, "DVE")
                dmaS(bass.AP(brow_d, 0, [[32 * 128, 1], [1, 32 * 128]]), FWS[0:1, :])

    with nc.Block():
        g.wait_ge(cx.sem["qL"], cx.cnt["qL"])
        ii = g.collective_compute("AllGather", ALU.bypass, replica_groups=[list(range(NCORES))],
                                  ins=[fold_in.ap().opt()], outs=[fold_out.ap().opt()])
        cx.inc(ii, "CC")
        cx.wait(sy, "qL")
        b1_ld = dmaS(GTHT[0:32, 0:128], brow_d.ap())
        dwait(t, b1_ld)
        bank_wait(t, 0)
        ii = t.transpose(PS[0:128, 0, 0:32], GTHT[0:32, 0:128], IDT[0:32, 0:32])
        pe_v = cx.inc(ii, "PE")
        cx.wait(v, "PE", pe_v)
        ii = v.tensor_copy(B1C[:], PS[:, 0, 0:32])
        cx.inc(ii, "DVE")
        bank_done(0)
        cx.wait(v, "DVE")
        ii = v.tensor_tensor(B1C[:], B1C[:], UBC[:], ALU.add)
        cx.inc(ii, "DVE")
        ii = v.tensor_tensor(B2C[:], B2C[:], DBC[:], ALU.add)
        cx.inc(ii, "DVE")
        cx.wait(v, "qL")

    def fm_ln(rms=False):
        pe_last = None
        for f in range(NF):
            xt = X[:, f, :]
            cx.wait(v, "PE")
            ii = v.tensor_copy(HIT[:], xt)
            cx.inc(ii, "DVE"); cx.wait(v, "DVE")
            ii = v.tensor_tensor(LOT[:], xt, HIT[:], ALU.subtract)
            cx.inc(ii, "DVE"); cx.wait(t, "DVE")
            for n in range(4):
                sl = slice(512 * n, 512 * (n + 1))
                if f == 0:
                    bank_wait(t, n)
                t.matmul(PS[:, n, :], ONESB[:, :], HIT[:, sl], start=(f == 0), stop=False)
                ii = t.matmul(PS[:, n, :], ONESB[:, :], LOT[:, sl], start=False, stop=(f == NF - 1))
            cx.inc(ii, "PE")
            ii = v.tensor_tensor(SCA[:], xt, xt, ALU.mult)
            cx.inc(ii, "DVE"); cx.wait(v, "DVE")
            cx.wait(v, "PE")
            ii = v.tensor_copy(HIT[:], SCA[:])
            cx.inc(ii, "DVE"); cx.wait(v, "DVE")
            ii = v.tensor_tensor(LOT[:], SCA[:], HIT[:], ALU.subtract)
            cx.inc(ii, "DVE"); cx.wait(t, "DVE")
            for n in range(4):
                sl = slice(512 * n, 512 * (n + 1))
                if f == 0:
                    bank_wait(t, 4 + n)
                t.matmul(PS[:, 4 + n, :], ONESB[:, :], HIT[:, sl], start=(f == 0), stop=False)
                ii = t.matmul(PS[:, 4 + n, :], ONESB[:, :], LOT[:, sl], start=False, stop=(f == NF - 1))
            pe_last = cx.inc(ii, "PE")
        cx.wait(v, "PE", pe_last)
        ii = v.tensor_scalar(WNB[:], PSW, ccol(1.0 / D), None, ALU.mult)
        cx.inc(ii, "DVE"); cx.wait(v, "DVE")
        ii = v.tensor_tensor(PSW, WNB[:], WNB[:], ALU.mult)
        cx.inc(ii, "DVE"); cx.wait(v, "DVE")
        ii = v.tensor_scalar(SCA[:], PSW2, ccol(1.0 / D), None, ALU.mult)
        cx.inc(ii, "DVE"); cx.wait(v, "DVE")
        ii = v.tensor_tensor(SCA[:], SCA[:], PSW, ALU.subtract)
        cx.inc(ii, "DVE"); cx.wait(s, "DVE")
        if rms:
            ii = s.activation(PSW2, SCA[:], AF.Sqrt, bias=ccol(0.0), scale=ccol(float(D)))
            cx.inc(ii, "ACT"); cx.wait(s, "ACT")
        ii = s.activation(PSW, SCA[:], AF.Sqrt, bias=ccol(1e-5), scale=ccol(1.0))
        cx.inc(ii, "ACT"); cx.wait(v, "ACT")
        ii = v.reciprocal(SCA[:], PSW)
        cx.inc(ii, "DVE"); cx.wait(v, "DVE")
        if rms:
            for hh, dst in ((0, HT1), (1, HT2)):
                hsl = slice(1024 * hh, 1024 * (hh + 1))
                pw = bass.AP(PS, 4 * 512 + 1024 * hh, [[8 * 512, 128], [1, 1024]])
                ii = v.tensor_tensor(dst[:], pw, SCA[:, hsl], ALU.mult)
                cx.inc(ii, "DVE"); cx.wait(v, "DVE")
                ii = v.tensor_scalar(dst[:], dst[:], ccol(1e-12), None, ALU.max)
                cx.inc(ii, "DVE"); cx.wait(v, "DVE")
                ii = v.reciprocal(dst[:], dst[:])
                cx.inc(ii, "DVE"); cx.wait(v, "DVE")
                ii = v.scalar_tensor_tensor(SCA[:, hsl], dst[:], ccol(32.0), SCA[:, hsl],
                                            ALU.mult, ALU.mult)
                cx.inc(ii, "DVE"); cx.wait(v, "DVE")
        ii = v.tensor_tensor(WNB[:], WNB[:], SCA[:], ALU.mult)
        cx.inc(ii, "DVE"); cx.wait(v, "DVE")
        ii = v.tensor_scalar(WNB[:], WNB[:], ccol(-1.0), None, ALU.mult)
        cx.inc(ii, "DVE"); cx.wait(v, "DVE")
        for f in range(NF):
            for hh, tmp in ((0, HT1), (1, HT2)):
                hsl = slice(1024 * hh, 1024 * (hh + 1))
                ii = v.tensor_tensor(tmp[:], X[:, f, hsl], SCA[:, hsl], ALU.mult)
                cx.inc(ii, "DVE"); cx.wait(v, "DVE")
                if rms:
                    ii = v.tensor_tensor(tmp[:], tmp[:], WNB[:, hsl], ALU.add)
                    cx.inc(ii, "DVE"); cx.wait(v, "DVE")
                    ii = v.tensor_scalar(ZB[:, f, hsl], tmp[:], GAM[:, f:f + 1], None, ALU.mult)
                    cx.inc(ii, "DVE"); cx.wait(v, "DVE")
                else:
                    ii = v.tensor_tensor(ZB[:, f, hsl], tmp[:], WNB[:, hsl], ALU.add)
                    cx.inc(ii, "DVE"); cx.wait(v, "DVE")
        for n in range(8):
            bank_done(n)
        cx.wait(t, "DVE")

    SW1 = ["sW1a", "sW1b", "sW1c"]
    SW3 = ["sW3a", "sW3b", "sW3c"]
    MW1 = ["mW1a", "mW1b", "mW1c"]
    MW3 = ["mW3a", "mW3b", "mW3c"]

    def stream_w1_slice(buf, m):
        ap = bass.AP(fold_out, 128 * m, [[H, 128], [2 * 128 * H, 8], [1, 128]])
        return dmaS(WS1[:, buf, :, :], ap, SW1[buf])

    def stream_w2_chunk(buf, m):
        ap = bass.AP(fold_out, 128 * H + 128 * 128 * m, [[128, 128], [2 * 128 * H, 8], [1, 128]])
        return dmaS(WS3[:, buf, :], ap, SW3[buf])

    def l0_ffn():
        fm_ln(rms=False)
        cx.wait(sy, "CC", 1)
        cx.wait(sy, "DVE"); cx.wait(sy, "PE")
        pe1_done = {}
        ld1 = {0: stream_w1_slice(0, 0), 1: stream_w1_slice(1, 1)}
        for step in range(2 * NKH):
            half, m = step // NKH, step % NKH
            buf = step % 3
            if step + 2 < 2 * NKH:
                if step >= 1:
                    sy.wait_ge(cx.sem["PE"], pe1_done[step - 1])
                ld1[step + 2] = stream_w1_slice((step + 2) % 3, (step + 2) % NKH)
            dwait(t, ld1[step])
            for n in range(2):
                bank = (2 * step + n) % 4
                sl = slice(half * TH + 512 * n, half * TH + 512 * (n + 1))
                bank_wait(t, bank)
                for k in range(8):
                    ii = t.matmul(PS[:, bank, :], WS1[:, buf, k, :], ZB[:, k, sl],
                                  start=(k == 0), stop=(k == 7))
                pe_v = cx.inc(ii, "PE")
                cx.wait(s, "PE", pe_v)
                ii = s.activation(HB[:, m, 512 * n:512 * (n + 1)], PS[:, bank, :], AF.Gelu,
                                  bias=B1C[:, m:m + 1], scale=ccol(1.0))
                av = cx.inc(ii, "ACT")
                bank_state[bank] = ("ACT", av)
            pe1_done[step] = cx.cnt["PE"]
            if m == NKH - 1:
                for sw in range(2):
                    sl_t = slice(half * TH + 512 * sw, half * TH + 512 * (sw + 1))
                    pe2_done = {}
                    sy.wait_ge(cx.sem["PE"], cx.cnt["PE"])
                    cx.wait(sy, "DVE")
                    ld2 = {0: stream_w2_chunk(0, 0), 1: stream_w2_chunk(1, 1)}
                    for m2 in range(NKH):
                        buf2 = m2 % 3
                        if m2 + 2 < NKH:
                            if m2 >= 1:
                                sy.wait_ge(cx.sem["PE"], pe2_done[m2 - 1])
                            ld2[m2 + 2] = stream_w2_chunk((m2 + 2) % 3, m2 + 2)
                        dwait(t, ld2[m2])
                        for dt in range(NF):
                            if m2 == 0:
                                bank_wait(t, dt)
                            ii = t.matmul(PS[:, dt, :], WS3[:, buf2, 128 * dt:128 * (dt + 1)],
                                          HB[:, m2, 512 * sw:512 * (sw + 1)],
                                          start=(m2 == 0), stop=(m2 == NKH - 1))
                        pe2_done[m2] = cx.inc(ii, "PE")
                    cx.wait(v, "PE", pe2_done[NKH - 1])
                    for grp in range(2):
                        for j in range(4):
                            dt = 4 * grp + j
                            ii = v.tensor_scalar(EVS[:, j, :], PS[:, dt, :], B2C[:, dt:dt + 1],
                                                 ccol(0.5), ALU.add, ALU.mult)
                        cx.inc(ii, "DVE"); cx.wait(v, "DVE")
                        for j in range(4):
                            dt = 4 * grp + j
                            ii = v.tensor_tensor(X[:, dt, sl_t], X[:, dt, sl_t], EVS[:, j, :], ALU.add)
                        cx.inc(ii, "DVE"); cx.wait(v, "DVE")
                        for j in range(4):
                            bank_done(4 * grp + j)
                    cx.wait(t, "DVE")

    def l0_attn():
        fm_ln(rms=True)
        SCa = SCA[:, 0:1024]; SCs = SCA[:, 1024:2048]
        SCk = WNB[:, 0:1024]; SCp = WNB[:, 1024:2048]
        ii = v.memset(SCO[:], 1.0)
        cx.inc(ii, "DVE")
        cx.wait(g, "DVE"); cx.wait(g, "PE")
        pe_q = {}
        qk_ld = {0: dmaW(WS1[:, 0, :, :],
                         bass.AP(dr["wqkvaT"], D, [[3 * D, 128], [128 * 3 * D, 8], [1, 128]]),
                         "mW1a"),
                 1: dmaW(WS1[:, 1, :, :],
                         bass.AP(dr["wqkvaT"], 2 * D, [[3 * D, 128], [128 * 3 * D, 8], [1, 128]]),
                         "mW1b")}
        for half in range(2):
            hs = slice(half * TH, (half + 1) * TH)
            for f in range(NF):
                cx.wait(s, "DVE")
                for qi in range(3):
                    stp = (half * NF + f) * 3 + qi
                    buf = stp % 3
                    if stp + 2 < 48:
                        nstp = stp + 2
                        nf = (nstp % 24) // 3
                        nqi = nstp % 3
                        nbase = [D + 128 * nf, 2 * D + 128 * nf, 128 * nf][nqi]
                        if stp >= 1:
                            g.wait_ge(cx.sem["PE"], pe_q[stp - 1])
                        qk_ld[nstp] = dmaW(WS1[:, nstp % 3, :, :],
                                           bass.AP(dr["wqkvaT"], nbase,
                                                   [[3 * D, 128], [128 * 3 * D, 8], [1, 128]]),
                                           MW1[nstp % 3])
                    dwait(t, qk_ld[stp])
                    for n in range(2):
                        bank = 2 * qi + n
                        sl = slice(half * TH + 512 * n, half * TH + 512 * (n + 1))
                        bank_wait(t, bank)
                        for k in range(8):
                            ii = t.matmul(PS[:, bank, :], WS1[:, buf, k, :], ZB[:, k, sl],
                                          start=(k == 0), stop=(k == 7))
                        pe_v = cx.inc(ii, "PE")
                        cx.wait(s, "PE", pe_v)
                        dsl = slice(512 * n, 512 * (n + 1))
                        if qi == 0:
                            ii = s.copy(SCk[:, dsl], PS[:, bank, :])
                        elif qi == 1:
                            ii = s.activation(SCa[:, dsl], PS[:, bank, :], AF.Sigmoid)
                        else:
                            ii = s.copy(SCQ2[:, dsl], PS[:, bank, :])
                        av = cx.inc(ii, "ACT")
                        bank_state[bank] = ("ACT", av)
                    pe_q[stp] = cx.cnt["PE"]
                cx.wait(v, "ACT")
                if half == 0:
                    ii = v.tensor_tensor_scan(SCs, SCa, SCk, 0.0, ALU.mult, ALU.add)
                else:
                    ii = v.tensor_tensor_scan(SCs, SCa, SCk, A1C[:, 8 + f:8 + f + 1],
                                              ALU.mult, ALU.add)
                cx.inc(ii, "DVE"); cx.wait(v, "DVE")
                ii = v.tensor_tensor_scan(SCp, SCa, SCO[:], 1.0, ALU.mult, ALU.mult)
                cx.inc(ii, "DVE"); cx.wait(v, "DVE")
                if half == 0:
                    v.tensor_copy(A1C[:, f:f + 1], SCp[:, 1023:1024])
                    ii = v.tensor_copy(A1C[:, 8 + f:8 + f + 1], SCs[:, 1023:1024])
                    cx.inc(ii, "DVE"); cx.wait(v, "DVE")
                else:
                    v.tensor_scalar(CARR[:, f:f + 1], SCp[:, 1023:1024], A1C[:, f:f + 1],
                                    None, ALU.mult)
                    ii = v.tensor_copy(CARR[:, 8 + f:8 + f + 1], SCs[:, 1023:1024])
                    cx.inc(ii, "DVE"); cx.wait(v, "DVE")
                ii = v.tensor_tensor(QTL[:, f, half, :], SCQ2[:], SCp, ALU.mult)
                cx.inc(ii, "DVE"); cx.wait(v, "DVE")
                ii = v.tensor_tensor(SCk, SCQ2[:], SCs, ALU.mult)
                cx.inc(ii, "DVE"); cx.wait(v, "DVE")
                ii = v.tensor_tensor(X[:, f, hs], X[:, f, hs], SCk, ALU.add)
                cx.inc(ii, "DVE"); cx.wait(v, "DVE")
        cx.wait(t, "DVE")
        bank_wait(t, 0)
        ii = t.transpose(PS[0:16, 0, 0:128], CARR[:, 0:16], IDT[:])
        pe_v = cx.inc(ii, "PE")
        cx.wait(v, "PE", pe_v)
        ii = v.tensor_copy(ATS[:], PS[0:16, 0, 0:128])
        cx.inc(ii, "DVE")
        bank_done(0)
        cx.wait(sy, "DVE")
        at_st = dmaS(att_in.ap(), ATS[:])
        g.wait_ge(cx.sem["qL"], at_st[1])
        ii = g.collective_compute("AllGather", ALU.bypass, replica_groups=[list(range(NCORES))],
                                  ins=[att_in.ap().opt()], outs=[att_out.ap().opt()])
        cc_v = cx.inc(ii, "CC")
        cx.wait(sy, "CC", cc_v)
        gt_ld = dmaS(GTHT[:], att_out.ap())
        dwait(t, gt_ld)
        bank_wait(t, 1)
        ii = t.transpose(PS[:, 1, 0:128], GTHT[:], IDT[:])
        pe_v = cx.inc(ii, "PE")
        cx.wait(v, "PE", pe_v)
        ii = v.tensor_copy(GTHT[:], PS[:, 1, 0:128])
        cx.inc(ii, "DVE")
        bank_done(1)
        cx.wait(v, "DVE")
        for f in range(NF):
            ii = v.memset(CC8[:, 0:1], 0.0)
            cx.inc(ii, "DVE"); cx.wait(v, "DVE")
            for r in range(1, 8):
                if r == 4:
                    ii = v.memset(CC8[:, 4:5], 0.0)
                    cx.inc(ii, "DVE"); cx.wait(v, "DVE")
                    continue
                ii = v.scalar_tensor_tensor(CC8[:, r:r + 1], CC8[:, r - 1:r],
                                            GTHT[:, 16 * (r - 1) + f:16 * (r - 1) + f + 1],
                                            GTHT[:, 16 * (r - 1) + 8 + f:16 * (r - 1) + 8 + f + 1],
                                            ALU.mult, ALU.add)
                cx.inc(ii, "DVE"); cx.wait(v, "DVE")
            ii = v.tensor_tensor(CC8[:], CC8[:], OHB[:], ALU.mult)
            cx.inc(ii, "DVE"); cx.wait(v, "DVE")
            ii = v.reduce_sum(CARR[:, f:f + 1], CC8[:], axis=AXX)
            cx.inc(ii, "DVE"); cx.wait(v, "DVE")
            ii = v.tensor_scalar(CARR[:, 8 + f:8 + f + 1], A1C[:, f:f + 1], CARR[:, f:f + 1],
                                 None, ALU.mult)
            cx.inc(ii, "DVE"); cx.wait(v, "DVE")
        for f in range(NF):
            for half in range(2):
                hc = CARR[:, 8 * half + f:8 * half + f + 1]
                ii = v.tensor_scalar(SCk, QTL[:, f, half, :], hc, None, ALU.mult)
                cx.inc(ii, "DVE"); cx.wait(v, "DVE")
                hs = slice(half * TH, (half + 1) * TH)
                ii = v.tensor_tensor(X[:, f, hs], X[:, f, hs], SCk, ALU.add)
                cx.inc(ii, "DVE"); cx.wait(v, "DVE")

    def cos_ops(dst, src, scr):
        ii = v.tensor_scalar(scr, src, ccol(1.0 / (2 * PI)), ccol(0.25), ALU.mult, ALU.add)
        cx.inc(ii, "DVE"); cx.wait(v, "DVE")
        ii = v.tensor_scalar(scr, scr, ccol(MAGIC), ccol(-MAGIC), ALU.add, ALU.add)
        cx.inc(ii, "DVE"); cx.wait(v, "DVE")
        ii = v.scalar_tensor_tensor(scr, scr, ccol(-2 * PI), src, ALU.mult, ALU.add)
        cx.inc(ii, "DVE"); cx.wait(s, "DVE")
        ii = s.activation(dst, scr, AF.Sin, bias=ccol(PI / 2), scale=ccol(1.0))
        cx.inc(ii, "ACT"); cx.wait(v, "ACT")

    def moe():
        fm_ln(rms=False)
        for qi in range(2):
            nb = 8 if qi == 0 else 4
            for n in range(4):
                src_bank = n % 2
                bank_wait(t, src_bank)
                sl = slice(512 * n, 512 * (n + 1))
                for k in range(8):
                    ii = t.matmul(PS[0:nb, src_bank, :], RWT[:, k, 8 * qi:8 * qi + nb], ZB[:, k, sl],
                                  start=(k == 0), stop=(k == 7))
                pe_v = cx.inc(ii, "PE")
                for c in range(4):
                    tt = 4 * n + c
                    cx.wait(v, "PE")
                    ii = v.tensor_scalar(RSTG[0:nb, 0:128], PS[0:nb, src_bank, 128 * c:128 * (c + 1)],
                                         RBC[0:nb, qi:qi + 1], None, ALU.add)
                    cx.inc(ii, "DVE"); cx.wait(t, "DVE")
                    tb = 2 + c % 2
                    bank_wait(t, tb)
                    ii = t.transpose(PS[0:128, tb, 0:nb], RSTG[0:nb, 0:128],
                                     IDT[0:nb, 0:nb])
                    pe_v2 = cx.inc(ii, "PE")
                    cx.wait(v, "PE", pe_v2)
                    dcols = slice(0, 8) if qi == 0 else slice(8, 12)
                    ii = v.tensor_copy(RLB[:, tt, dcols], PS[:, tb, 0:nb])
                    cx.inc(ii, "DVE")
                    bank_done(tb)
                bank_done(src_bank)

        def batch(op, sem="DVE"):
            last = None
            for tt in range(16):
                last = op(tt)
            cx.inc(last, sem)
            cx.wait(v, sem); cx.wait(s, sem)

        cx.wait(v, "DVE")
        batch(lambda tt: v.reduce_max(RLB[:, tt, 12:13], RLB[:, tt, 8:12], axis=AXX))
        batch(lambda tt: v.tensor_scalar(RLB[:, tt, 8:12], RLB[:, tt, 8:12], RLB[:, tt, 12:13],
                                         None, ALU.subtract))
        batch(lambda tt: s.activation(RLB[:, tt, 8:12], RLB[:, tt, 8:12], AF.Exp), "ACT")
        batch(lambda tt: v.reduce_sum(RLB[:, tt, 12:13], RLB[:, tt, 8:12], axis=AXX))
        batch(lambda tt: v.reciprocal(RLB[:, tt, 12:13], RLB[:, tt, 12:13]))
        batch(lambda tt: v.tensor_scalar(RLB[:, tt, 8:12], RLB[:, tt, 8:12], RLB[:, tt, 12:13],
                                         None, ALU.mult))
        batch(lambda tt: v.tensor_tensor(RLB[:, tt, 12:15], RLB[:, tt, 8:11], RLB[:, tt, 9:12], ALU.add))
        batch(lambda tt: v.tensor_copy(RLB[:, tt, 15:16], RLB[:, tt, 11:12]))
        batch(lambda tt: v.scalar_tensor_tensor(RLB[:, tt, 0:4], RLB[:, tt, 12:16], ccol(0.1),
                                                RLB[:, tt, 0:4], ALU.mult, ALU.add))
        batch(lambda tt: v.scalar_tensor_tensor(RLB[:, tt, 4:8], RLB[:, tt, 12:16], ccol(0.1),
                                                RLB[:, tt, 4:8], ALU.mult, ALU.add))
        batch(lambda tt: v.reduce_max(RLB[:, tt, 12:13], RLB[:, tt, 0:8], axis=AXX))
        batch(lambda tt: v.tensor_scalar(RLB[:, tt, 0:8], RLB[:, tt, 0:8], RLB[:, tt, 12:13],
                                         None, ALU.subtract))
        batch(lambda tt: s.activation(RLB[:, tt, 0:8], RLB[:, tt, 0:8], AF.Exp), "ACT")
        batch(lambda tt: v.reduce_sum(RLB[:, tt, 12:13], RLB[:, tt, 0:8], axis=AXX))
        batch(lambda tt: v.reciprocal(RLB[:, tt, 12:13], RLB[:, tt, 12:13]))
        batch(lambda tt: v.tensor_scalar(RLB[:, tt, 0:8], RLB[:, tt, 0:8], RLB[:, tt, 12:13],
                                         None, ALU.mult))
        for tt in range(16):
            ii = v.max(BNS[:, 0:8], RLB[:, tt, 0:8])
            cx.inc(ii, "DVE"); cx.wait(v, "DVE")
            ii = v.tensor_scalar(WNM[:, tt, :], RLB[:, tt, 0:8], BNS[:, 1:2], None, ALU.is_ge)
            cx.inc(ii, "DVE"); cx.wait(v, "DVE")
        batch(lambda tt: v.tensor_tensor(WNM[:, tt, :], WNM[:, tt, :], RLB[:, tt, 0:8], ALU.mult))
        batch(lambda tt: v.reduce_sum(RLB[:, tt, 12:13], WNM[:, tt, :], axis=AXX))
        batch(lambda tt: v.reciprocal(RLB[:, tt, 12:13], RLB[:, tt, 12:13]))
        batch(lambda tt: v.tensor_scalar(WNM[:, tt, :], WNM[:, tt, :], RLB[:, tt, 12:13], ccol(0.5),
                                         ALU.mult, ALU.mult))
        cx.wait(t, "DVE")
        wn_st = 0
        for tt in range(16):
            bank = tt % 2
            bank_wait(t, bank)
            ii = t.transpose(PS[0:8, bank, 0:128], WNM[:, tt, :], IDT[:])
            pe_v = cx.inc(ii, "PE")
            cx.wait(v, "PE", pe_v)
            cx.wait(v, "qL")
            ii = v.tensor_copy(WNT8[:], PS[0:8, bank, 0:128])
            cx.inc(ii, "DVE")
            bank_done(bank)
            cx.wait(sy, "DVE")
            wn_st = dmaS(bass.AP(wnrows, 128 * tt, [[T, 8], [1, 128]]), WNT8[:])
        pe1_done = {}
        cx.wait(g, "DVE"); cx.wait(g, "PE")
        w1ld = {0: dmaW(WS1[:, 0, :, :],
                        dr["moe_w1_r"].ap()[0, 0].rearrange("(k p) m -> p k m", p=128), "mW1a"),
                1: dmaW(WS1[:, 1, :, :],
                        dr["moe_w1_r"].ap()[0, 1].rearrange("(k p) m -> p k m", p=128), "mW1b")}
        for e in range(NE):
            dwait(sy, wn_st)
            cx.wait(sy, "DVE")
            wnb_ld = dmaS(WNB[:], bass.AP(wnrows, e * T, [[0, 128], [1, T]]))
            for half in range(2):
                ts0 = half * TH
                for m in range(NKH):
                    stp = (2 * e + half) * NKH + m
                    buf = stp % 3
                    if stp + 2 < 2 * NE * NKH:
                        ne_ = (stp + 2) // (2 * NKH)
                        nm_ = (stp + 2) % NKH
                        if stp >= 1:
                            g.wait_ge(cx.sem["PE"], pe1_done[stp - 1])
                        w1ld[stp + 2] = dmaW(WS1[:, (stp + 2) % 3, :, :],
                                             dr["moe_w1_r"].ap()[ne_, nm_].rearrange(
                                                 "(k p) m -> p k m", p=128),
                                             MW1[(stp + 2) % 3])
                    dwait(t, w1ld[stp])
                    for n in range(2):
                        bank = (2 * stp + n) % 4
                        sl = slice(ts0 + 512 * n, ts0 + 512 * (n + 1))
                        bank_wait(t, bank)
                        for k in range(8):
                            ii = t.matmul(PS[:, bank, :], WS1[:, buf, k, :], ZB[:, k, sl],
                                          start=(k == 0), stop=(k == 7))
                        pe_v = cx.inc(ii, "PE")
                        cx.wait(s, "PE", pe_v)
                        ii = s.activation(HB[:, m, 512 * n:512 * (n + 1)], PS[:, bank, :], AF.Gelu,
                                          bias=MB1[:, NKH * e + m:NKH * e + m + 1], scale=ccol(1.0))
                        av = cx.inc(ii, "ACT")
                        bank_state[bank] = ("ACT", av)
                    pe1_done[stp] = cx.cnt["PE"]
                for sw in range(2):
                    sl_t = slice(ts0 + 512 * sw, ts0 + 512 * (sw + 1))
                    pe2_done = {}
                    g.wait_ge(cx.sem["PE"], cx.cnt["PE"])
                    w2ld = {0: dmaW(WS3[:, 0, :], dr["moe_w2"].ap()[e, 0:128, :], "mW3a"),
                            1: dmaW(WS3[:, 1, :], dr["moe_w2"].ap()[e, 128:256, :], "mW3b")}
                    for m2 in range(NKH):
                        buf2 = m2 % 3
                        if m2 + 2 < NKH:
                            if m2 >= 1:
                                g.wait_ge(cx.sem["PE"], pe2_done[m2 - 1])
                            w2ld[m2 + 2] = dmaW(WS3[:, (m2 + 2) % 3, :],
                                                dr["moe_w2"].ap()[e, 128 * (m2 + 2):128 * (m2 + 3), :],
                                                MW3[(m2 + 2) % 3])
                        dwait(t, w2ld[m2])
                        if m2 == 0:
                            dwait(t, wnb_ld)
                        for dt in range(NF):
                            if m2 == 0:
                                bank_wait(t, dt)
                            ii = t.matmul(PS[:, dt, :], WS3[:, buf2, 128 * dt:128 * (dt + 1)],
                                          HB[:, m2, 512 * sw:512 * (sw + 1)],
                                          start=(m2 == 0), stop=(m2 == NKH - 1))
                        pe2_done[m2] = cx.inc(ii, "PE")
                    cx.wait(v, "PE", pe2_done[NKH - 1])
                    for grp in range(2):
                        for j in range(4):
                            dt = 4 * grp + j
                            ii = v.scalar_tensor_tensor(EVS[:, j, :], PS[:, dt, :],
                                                        MB2C[:, NF * e + dt:NF * e + dt + 1],
                                                        WNB[:, sl_t], ALU.add, ALU.mult)
                        cx.inc(ii, "DVE"); cx.wait(v, "DVE")
                        for j in range(4):
                            dt = 4 * grp + j
                            ii = v.tensor_tensor(X[:, dt, sl_t], X[:, dt, sl_t], EVS[:, j, :], ALU.add)
                        cx.inc(ii, "DVE"); cx.wait(v, "DVE")
                        for j in range(4):
                            bank_done(4 * grp + j)
                    cx.wait(t, "DVE")

    def tm_ln_stats(srcv):
        v.bn_stats(BNS[:, 0:6], srcv[:, 0:512])
        ii = v.bn_stats(BNS[:, 6:12], srcv[:, 512:1024])
        cx.inc(ii, "DVE"); cx.wait(v, "DVE")
        ii = v.bn_aggr(BNA[:, 0:2], BNS[:, 0:12])
        cx.inc(ii, "DVE"); cx.wait(s, "DVE")
        ii = s.activation(BNA[:, 2:3], BNA[:, 1:2], AF.Sqrt, bias=ccol(1e-5), scale=ccol(1.0))
        cx.inc(ii, "ACT"); cx.wait(v, "ACT")
        ii = v.reciprocal(BNA[:, 2:3], BNA[:, 2:3])
        cx.inc(ii, "DVE"); cx.wait(v, "DVE")

    def l1_attn():
        TZ = SCA[:, 0:1024]; ZN = SCA[:, 1024:2048]
        CP1 = WNB[:, 0:1024]; CP2 = WNB[:, 1024:2048]
        xs = None
        for f in range(NF):
            xs = dmaST(bass.AP(xspill, f * 128 * T, [[T, 128], [1, T]]), X[:, f, :])
        cx.wait(sy, "ST", xs)
        bb_ld = dmaS(B2BP[:], bass.AP(dr["b2_l1a"], 0, [[0, 128], [1, D]]))
        g1_ld = dmaS(G1B[:], bass.AP(dr["gamma1"], 0, [[0, 128], [1, D]]))
        be_ld = dmaS(ST5[:], bass.AP(dr["beta1"], 0, [[0, 128], [1, D]]))
        dwait(v, be_ld)
        ii = v.tensor_tensor(B2BP[:], B2BP[:], ST5[:], ALU.add)
        cx.inc(ii, "DVE"); cx.wait(v, "DVE")
        v.tensor_copy(Q16[:, 0, :], G1B[:, 0:16])
        ii = v.tensor_copy(Q16[:, 1, :], ST5[:, 0:16])
        cx.inc(ii, "DVE"); cx.wait(v, "DVE")
        for tt in range(16):
            cx.wait(sy, "DVE")
            xc_ld = dmaS(bass.AP(WNB, 0, [[2048, 128], [128, 8], [1, 128]]),
                         bass.AP(xspill, 128 * tt, [[T, 128], [128 * T, 8], [1, 128]]), "xcl")
            dwait(t, xc_ld)
            for f in range(NF):
                bank = f % 2
                bank_wait(t, bank)
                ii = t.transpose(PS[:, bank, 0:128], CP1[:, 128 * f:128 * (f + 1)], IDT[:])
                pe_v = cx.inc(ii, "PE")
                cx.wait(v, "PE", pe_v)
                ii = v.tensor_copy(TZ[:, 128 * f:128 * (f + 1)], PS[:, bank, 0:128])
                cx.inc(ii, "DVE")
                bank_done(bank)
            cx.wait(v, "DVE")
            tm_ln_stats(TZ)
            ii = v.tensor_scalar(ZN, TZ, BNA[:, 0:1], BNA[:, 2:3], ALU.subtract, ALU.mult)
            cx.inc(ii, "DVE"); cx.wait(v, "DVE")
            cos_ops(CP1, ZN, CP2)
            src, dst = CP1, CP2
            for sh in (1, 2, 4, 8, 16, 32):
                v.tensor_copy(bass.AP(WNB, dst.offset, [[2048, 128], [64, 16], [1, sh]]),
                              bass.AP(WNB, src.offset, [[2048, 128], [64, 16], [1, sh]]))
                ii = v.tensor_tensor(
                    bass.AP(WNB, dst.offset + sh, [[2048, 128], [64, 16], [1, 64 - sh]]),
                    bass.AP(WNB, src.offset + sh, [[2048, 128], [64, 16], [1, 64 - sh]]),
                    bass.AP(WNB, src.offset, [[2048, 128], [64, 16], [1, 64 - sh]]),
                    ALU.mult)
                cx.inc(ii, "DVE"); cx.wait(v, "DVE")
                src, dst = dst, src
            ii = v.tensor_tensor(TZ, ZN, src, ALU.add)
            cx.inc(ii, "DVE"); cx.wait(v, "DVE")
            tm_ln_stats(TZ)
            ii = v.tensor_scalar(X1[:, tt, :], TZ, BNA[:, 0:1], BNA[:, 2:3], ALU.subtract, ALU.mult)
            cx.inc(ii, "DVE"); cx.wait(v, "DVE")
            ii = v.tensor_tensor(CP1[:, 0:16], X1[:, tt, 0:16], Q16[:, 0, :], ALU.mult)
            cx.inc(ii, "DVE"); cx.wait(v, "DVE")
            ii = v.tensor_tensor(CP1[:, 0:16], CP1[:, 0:16], Q16[:, 1, :], ALU.add)
            cx.inc(ii, "DVE"); cx.wait(v, "DVE")
            cos_ops(CP2[:, 0:16], CP1[:, 0:16], CP1[:, 16:32])
            src2, dst2 = CP2, CP1
            for sh in (1, 2, 4, 8):
                v.tensor_copy(dst2[:, 0:sh], src2[:, 0:sh])
                ii = v.tensor_tensor(dst2[:, sh:16], src2[:, sh:16], src2[:, 0:16 - sh], ALU.mult)
                cx.inc(ii, "DVE"); cx.wait(v, "DVE")
                src2, dst2 = dst2, src2
            ii = v.tensor_copy(QTM[:, tt, :], src2[:, 0:16])
            cx.inc(ii, "DVE"); cx.wait(v, "DVE")
        cx.wait(t, "DVE")
        for tt in range(16):
            bank = tt % 2
            bank_wait(t, bank)
            ii = t.transpose(PS[0:16, bank, 0:128], QTM[:, tt, :], IDT[:])
            pe_v = cx.inc(ii, "PE")
            cx.wait(v, "PE", pe_v)
            ii = v.tensor_copy(WQT[:, 128 * tt:128 * (tt + 1)], PS[0:16, bank, 0:128])
            cx.inc(ii, "DVE")
            bank_done(bank)
        cx.wait(g, "DVE"); cx.wait(g, "PE")
        w1_ld = {0: dmaW(W1S[:, 0, :], dr["w1T_l1a"].ap()[:, 0:128], "mW1a")}
        pe1v = {}
        for q in range(4):
            qs = slice(512 * q, 512 * (q + 1))
            cx.wait(t, "DVE")
            for m in range(NKH):
                stp = q * NKH + m
                buf = stp % 2
                if stp + 1 < 4 * NKH:
                    nm_ = (stp + 1) % NKH
                    if stp >= 1:
                        g.wait_ge(cx.sem["PE"], pe1v[stp - 1])
                    w1_ld[stp + 1] = dmaW(W1S[:, (stp + 1) % 2, :],
                                          dr["w1T_l1a"].ap()[:, 128 * nm_:128 * (nm_ + 1)],
                                          "mW1a" if (stp + 1) % 2 == 0 else "mW1b")
                dwait(t, w1_ld[stp])
                bank = m % 4
                bank_wait(t, bank)
                ii = t.matmul(PS[:, bank, :], W1S[:, buf, :], WQT[:, qs], start=True, stop=True)
                pe_v = cx.inc(ii, "PE")
                pe1v[stp] = pe_v
                cx.wait(s, "PE", pe_v)
                ii = s.activation(HBL[:, m, 0:512], PS[:, bank, :], AF.Relu,
                                  bias=B1CL[:, m:m + 1], scale=ccol(1.0))
                av = cx.inc(ii, "ACT")
                bank_state[bank] = ("ACT", av)
            pe2_done = {}
            g.wait_ge(cx.sem["PE"], cx.cnt["PE"])
            w2ld = {0: dmaW(WS3[:, 0, :], dr["w2T_l1a"].ap()[0:128, :], "mW3a"),
                    1: dmaW(WS3[:, 1, :], dr["w2T_l1a"].ap()[128:256, :], "mW3b")}
            for m2 in range(NKH):
                buf2 = m2 % 3
                if m2 + 2 < NKH:
                    if m2 >= 1:
                        g.wait_ge(cx.sem["PE"], pe2_done[m2 - 1])
                    w2ld[m2 + 2] = dmaW(WS3[:, (m2 + 2) % 3, :],
                                        dr["w2T_l1a"].ap()[128 * (m2 + 2):128 * (m2 + 3), :],
                                        MW3[(m2 + 2) % 3])
                dwait(t, w2ld[m2])
                for ttq in range(4):
                    for n in range(2):
                        bank = 2 * ttq + n
                        if m2 == 0:
                            bank_wait(t, bank)
                        ii = t.matmul(PS[:, bank, :], HBL[:, m2, 128 * ttq:128 * (ttq + 1)],
                                      WS3[:, buf2, 512 * n:512 * (n + 1)],
                                      start=(m2 == 0), stop=(m2 == NKH - 1))
                pe2_done[m2] = cx.inc(ii, "PE")
            cx.wait(v, "PE", pe2_done[NKH - 1])
            for ttq in range(4):
                tt = 4 * q + ttq
                ii = v.tensor_tensor(TZ, X1[:, tt, :], G1B[:], ALU.mult)
                cx.inc(ii, "DVE"); cx.wait(v, "DVE")
                ii = v.tensor_tensor(TZ, TZ, B2BP[:], ALU.add)
                cx.inc(ii, "DVE"); cx.wait(v, "DVE")
                for n in range(2):
                    bank = 2 * ttq + n
                    dsl = slice(512 * n, 512 * (n + 1))
                    ii = v.tensor_tensor(TZ[:, dsl], TZ[:, dsl], PS[:, bank, :], ALU.add)
                cx.inc(ii, "DVE"); cx.wait(v, "DVE")
                bank_done(2 * ttq); bank_done(2 * ttq + 1)
                tm_ln_stats(TZ)
                ii = v.tensor_scalar(ZN, TZ, BNA[:, 0:1], BNA[:, 2:3], ALU.subtract, ALU.mult)
                cx.inc(ii, "DVE"); cx.wait(t, "DVE")
                cx.wait(v, "stf")
                for f in range(NF):
                    bank = f % 2
                    bank_wait(t, bank)
                    ii = t.transpose(PS[:, bank, 0:128], ZN[:, 128 * f:128 * (f + 1)], IDT[:])
                    pe_v = cx.inc(ii, "PE")
                    cx.wait(v, "PE", pe_v)
                    ii = v.tensor_scalar(ST5[:, 128 * f:128 * (f + 1)], PS[:, bank, 0:128],
                                         G2C[:, f:f + 1], BE2C[:, f:f + 1], ALU.mult, ALU.add)
                    cx.inc(ii, "DVE")
                    bank_done(bank)
                cx.wait(sy, "DVE")
                dmaS(bass.AP(attn_fm, 128 * tt, [[T, 128], [128 * T, 8], [1, 128]]),
                     bass.AP(ST5, 0, [[1024, 128], [128, 8], [1, 128]]), "stf")
        cx.wait(sy, "stf")
        for f in range(NF):
            if f > 0:
                cx.wait(sy, "xcl")
                cx.wait(sy, "DVE")
            dmaS(X[:, f, :], bass.AP(xspill, f * 128 * T, [[T, 128], [1, T]]), "xcl")
            a_ld = dmaS(SCA[:], bass.AP(attn_fm, f * 128 * T, [[T, 128], [1, T]]), "xcl")
            dwait(v, a_ld)
            ii = v.tensor_tensor(X[:, f, :], X[:, f, :], SCA[:], ALU.add)
            cx.inc(ii, "DVE"); cx.wait(v, "DVE")

    if bp >= 1:
        with nc.Block():
            l0_ffn()
    if bp >= 2:
        with nc.Block():
            l0_attn()
    if bp >= 3:
        with nc.Block():
            l0_ffn()
    if bp >= 4:
        with nc.Block():
            moe()
    if bp >= 5:
        with nc.Block():
            l1_attn()
    if bp >= 6:
        with nc.Block():
            moe()

    with nc.Block():
        stv = None
        for f in range(NF):
            stv = dmaST(bass.AP(yT, f * 128 * T, [[T, 128], [1, T]]), X[:, f, :])
        g.wait_ge(cx.sem["ST"], stv)

    stack.close()
    return nc


_CACHE = {}


def kernel(**inputs):
    per_core = prep_inputs(inputs)
    key = BUILD_PHASES
    if key not in _CACHE:
        _CACHE[key] = build_program()
    nc = _CACHE[key]
    res = run_bass_kernel_spmd(nc, per_core, core_ids=list(range(NCORES)))
    out = np.empty((2, 4 * T, D), np.float32)
    for c in range(NCORES):
        b, q = c // 4, c % 4
        out[b, q * T:(q + 1) * T, :] = res.results[c]["yT"].T
    return out
